# version 05
# baseline (speedup 1.0000x reference)
import numpy as np
from contextlib import ExitStack

import concourse.bass as bass
import concourse.mybir as mybir

F32 = mybir.dt.float32
BF16 = mybir.dt.bfloat16
I32 = mybir.dt.int32
AF = mybir.ActivationFunctionType
ALU = mybir.AluOpType

D, N, II = 2048, 16, 8192
WARM, TOK = 256, 2048
EC = D // 128
IC = II // 128
TILES = [(0, 768, 256), (768, 512, 0), (1280, 512, 0), (1792, 512, 0)]
NT = len(TILES)


def _rngs(wm):
    return [(0, wm), (wm, 512)] if wm else [(0, 512)]


class Sem:
    def __init__(self, nc, ctx, name, table):
        self.h = ctx.enter_context(nc.semaphore(name))
        self.name, self.n, self.keys, self.table = name, 0, {}, table

    def inc(self, instr, key=None, q=1):
        self.n += q
        if key is not None:
            assert key not in self.keys, (self.name, key)
            self.keys[key] = self.n
        instr.then_inc(self.h, q)

    def mark(self, key):
        self.keys[key] = self.n

    def at(self, key):
        src = self.table[self.name] if self.table is not None else self.keys
        return src.get(key)

    def wait(self, eng, key, required=True):
        v = self.at(key)
        if v is None:
            if required and self.table is not None:
                raise KeyError(f"sem {self.name}: missing key {key}")
            return
        eng.wait_ge(self.h, v)


SEM_NAMES = [
    "s_const", "s_uch0", "s_uch1", "s_ubf", "s_ws0", "s_ws1",
    "s_pi0", "s_pi1", "s_ps", "s_pog0", "s_pog1", "s_pou0", "s_pou1", "s_pod0", "s_pod1",
    "s_sinit", "s_vinit",
    "s_prj", "s_sp", "s_w", "s_cE", "s_ab", "s_scan", "s_y", "s_r",
    "s_sq", "s_st", "s_rv", "s_rs", "s_bc", "s_x1", "s_x1b", "s_x1w0", "s_x1w1", "s_x1w2", "s_x1w3", "s_dbg0", "s_dbg1",
    "s_fx", "s_fgs0", "s_fgs1", "s_fus0", "s_fus1", "s_fds0", "s_fds1", "s_fg", "s_fu", "s_fsg", "s_fh",
    "s_fdm", "s_fr2", "s_fr2w0", "s_fr2w1", "s_fx1r0", "s_fx1r1",
    "s_fr2r0", "s_fr2r1", "s_fsq", "s_fst",
    "s_frv", "s_frs", "s_fbc", "s_fx2", "s_fo0", "s_fo1",
    "s_vself", "s_sself",
]

PREP_CHUNKS = ([("g", i) for i in range(EC)] + [("u", i) for i in range(EC)]
               + [("d", i) for i in range(IC)])
NPREP = len(PREP_CHUNKS)


def _po_sem(pi):
    return f"s_po{PREP_CHUNKS[pi][0]}{pi % 2}"
NCONSTD = 12


def build_nc(stage=3, table=None):
    nc = bass.Bass("TRN2", target_bir_lowering=False)

    u_in = nc.dram_tensor("u", [D, WARM + TOK], F32, kind="ExternalInput").ap()
    wmix = nc.dram_tensor("wmix", [D, 3 * EC * 128], F32, kind="ExternalInput").ap()
    bdt = nc.dram_tensor("bdt", [128, EC], F32, kind="ExternalInput").ap()
    bbi = nc.dram_tensor("bb", [128, EC], F32, kind="ExternalInput").ap()
    bci = nc.dram_tensor("bc", [128, EC], F32, kind="ExternalInput").ap()
    araw = nc.dram_tensor("araw", [128, EC * N], F32, kind="ExternalInput").ap()
    bbase = nc.dram_tensor("bbase", [128, EC * N], F32, kind="ExternalInput").ap()
    cbase = nc.dram_tensor("cbase", [128, EC * N], F32, kind="ExternalInput").ap()
    dvec = nc.dram_tensor("dvec", [128, EC], F32, kind="ExternalInput").ap()
    cin = nc.dram_tensor("cin", [1, 8], F32, kind="ExternalInput").ap()
    ccol = nc.dram_tensor("ccol", [128, 2], F32, kind="ExternalInput").ap()
    gw = nc.dram_tensor("gw", [D, II], I32, kind="ExternalInput").ap()
    uwq = nc.dram_tensor("uw", [D, II], I32, kind="ExternalInput").ap()
    dwq = nc.dram_tensor("dw", [II, D], I32, kind="ExternalInput").ap()
    gsi = nc.dram_tensor("gs", [128, IC], F32, kind="ExternalInput").ap()
    usi = nc.dram_tensor("us", [128, IC], F32, kind="ExternalInput").ap()
    dsi = nc.dram_tensor("ds", [128, EC], F32, kind="ExternalInput").ap()

    out = nc.dram_tensor("out", [D, TOK], F32, kind="ExternalOutput").ap()

    gwbT = nc.dram_tensor("gwbT", [D, II], BF16, kind="Internal").ap()
    uwbT = nc.dram_tensor("uwbT", [D, II], BF16, kind="Internal").ap()
    dwbT = nc.dram_tensor("dwbT", [II, D], BF16, kind="Internal").ap()
    x1f = nc.dram_tensor("x1f", [D, TOK], F32, kind="Internal").ap()
    x1bd = nc.dram_tensor("x1bd", [D, TOK], BF16, kind="Internal").ap()
    r2f = nc.dram_tensor("r2f", [D, TOK], F32, kind="Internal").ap()

    with ExitStack() as ctx0:
        csb = lambda nm, sh, dt: ctx0.enter_context(nc.sbuf_tensor(nm, sh, dt))
        sm = {name: Sem(nc, ctx0, name, table) for name in SEM_NAMES}

        cst = csb("cst", [1, 8], F32)
        nc.const_aps.register(F32, 1e-5, cst[0:1, 0:1])
        nc.const_aps.register(F32, -0.5, cst[0:1, 1:2])
        ccol_t = csb("ccol_t", [128, 2], F32)
        nc.const_aps.register(F32, -128.0, ccol_t[:, 0:1])
        eps4_t = ccol_t
        bdt_t = csb("bdt_t", [128, EC], F32)
        bb_t = csb("bb_t", [128, EC], F32)
        bc_t = csb("bc_t", [128, EC], F32)
        araw_t = csb("araw_t", [128, EC * N], F32)
        A_t = csb("A_t", [128, EC * N], F32)
        spl_t = csb("spl_t", [128, EC * N], F32)
        cbb_t = csb("cbb_t", [128, EC * N], F32)
        cb2_t = csb("cb2_t", [128, EC * N], F32)
        dp1_t = csb("dp1_t", [128, EC], F32)
        gs_t = csb("gs_t", [128, IC], F32)
        us_t = csb("us_t", [128, IC], F32)
        ds_t = csb("ds_t", [128, EC], F32)
        onesc_t = csb("onesc_t", [128, 1], F32)
        onesr_t = csb("onesr_t", [1, 128], F32)
        carry = csb("carry", [128, EC * N], F32)
        mrow = csb("mrow", [1, 512], F32)
        vrow = csb("vrow", [1, 512], F32)

        with ExitStack() as cm:
            msb = lambda nm, sh, dt: cm.enter_context(nc.sbuf_tensor(nm, sh, dt))
            mps = lambda nm, sh: cm.enter_context(nc.psum_tensor(nm, sh, F32))
            S0 = TILES[0][1]
            u_bf = msb("u_bf", [128, EC * S0], BF16)
            u_bf3 = u_bf[:].rearrange("p (k s) -> p k s", k=EC)
            u_ch = [msb(f"u_ch{i}", [128, S0], F32) for i in range(2)]
            dt_b = [msb(f"dt_b{i}", [128, S0], F32) for i in range(2)]
            w_b = [msb(f"w_b{i}", [128, S0], F32) for i in range(2)]
            c_b = [msb(f"c_b{i}", [128, 512], F32) for i in range(2)]
            spA = [msb(f"spA{i}", [128, S0], F32) for i in range(2)]
            ab_b = [msb(f"ab_b{i}", [128, S0], F32) for i in range(4)]
            xp_b = [msb(f"xp_b{i}", [128, S0], F32) for i in range(2)]
            ya_b = [msb(f"ya_b{i}", [128, 512], F32) for i in range(2)]
            r_t = msb("r_t", [128, EC * 512], F32)
            r_t3 = r_t[:].rearrange("p (k s) -> p k s", k=EC)
            x1_b = [msb(f"x1_b{i}", [128, 512], F32) for i in range(2)]
            x1c_b = [msb(f"x1c_b{i}", [128, 512], BF16) for i in range(2)]
            wstat = [msb(f"wstat{i}", [128, 3 * EC * 128], BF16) for i in range(2)]
            wstat4 = [w[:].rearrange("p (j k e) -> p j k e", j=3, k=EC) for w in wstat]
            pstg = [msb(f"pstg{i}", [128, II], BF16) for i in range(2)]

            psP = [mps(f"psP{j}", [128, 512]) for j in range(3)]
            psSm = mps("psSm", [1, 512])
            psSq = mps("psSq", [1, 512])
            psX = mps("psX", [128, 512])
            psIV = mps("psIV", [128, 512])

            with nc.Block() as blk:
                @blk.sync
                def _(sy):
                    for dst, src in [
                        (cst[:], cin), (bdt_t[:], bdt), (bb_t[:], bbi),
                        (bc_t[:], bci), (araw_t[:], araw), (cbb_t[:], bbase),
                        (cb2_t[:], cbase), (dp1_t[:], dvec), (gs_t[:], gsi),
                        (us_t[:], usi), (ds_t[:], dsi), (ccol_t[:], ccol),
                    ]:
                        sm["s_const"].inc(sy.dma_start(out=dst, in_=src), q=16)
                    sm["s_const"].mark("consts")
                    assert sm["s_const"].n == 16 * NCONSTD
                    po_box = [0]

                    def prep_out():
                        pi = po_box[0]
                        if pi >= NPREP:
                            return
                        mat, blki = PREP_CHUNKS[pi]
                        dst = {"g": gwbT, "u": uwbT, "d": dwbT}[mat]
                        w_ = II if mat in ("g", "u") else D
                        sm["s_ps"].wait(sy, ("sub", pi))
                        sm[_po_sem(pi)].inc(
                            sy.dma_start(out=dst[blki * 128:(blki + 1) * 128, :],
                                         in_=pstg[pi % 2][:, 0:w_]),
                            key=("po", pi), q=16)
                        po_box[0] += 1

                    for t, (u0, St, wm) in enumerate(TILES):
                        for ec in range(EC):
                            ge = t * EC + ec
                            if ge >= 2:
                                pt, pe = divmod(ge - 2, EC)
                                sm["s_r"].wait(sy, ("r", pt, pe))
                            sm[f"s_uch{ec % 2}"].inc(
                                sy.dma_start(
                                    out=u_ch[ec % 2][:, 0:St],
                                    in_=u_in[ec * 128:(ec + 1) * 128, u0:u0 + St]),
                                key=("uch", t, ec), q=16)
                            for _ in range(6):
                                prep_out()
                    while po_box[0] < NPREP:
                        prep_out()
                    for nmm in ("s_pog0", "s_pog1", "s_pou0", "s_pou1",
                                "s_pod0", "s_pod1"):
                        sm[nmm].mark("fin")

                @blk.gpsimd
                def _(gp):
                    pi_box = [0]

                    def prep_one():
                        pi = pi_box[0]
                        if pi >= NPREP:
                            return
                        mat, blki = PREP_CHUNKS[pi]
                        src = {"g": gw, "u": uwq, "d": dwq}[mat]
                        w_ = II if mat in ("g", "u") else D
                        if pi >= 2:
                            sm[_po_sem(pi - 2)].wait(gp, ("po", pi - 2))
                        sm[f"s_pi{pi % 2}"].inc(
                            gp.dma_start(out=pstg[pi % 2][:, 0:w_],
                                         in_=src[blki * 128:(blki + 1) * 128, :]),
                            key=("pi", pi), q=16)
                        pi_box[0] += 1

                    for t, (u0, St, wm) in enumerate(TILES):
                        if t > 0:
                            sm["s_prj"].wait(gp, ("last", t - 1, EC - 1))
                        sm["s_ubf"].inc(
                            gp.dma_start(
                                out=u_bf3[:, :, 0:St],
                                in_=u_in[:, u0:u0 + St]
                                .rearrange("(k p) s -> p k s", p=128)),
                            key=("ubf", t), q=16)
                        for ec in range(EC):
                            ge = t * EC + ec
                            if ge >= 2:
                                pt, pe = divmod(ge - 2, EC)
                                sm["s_prj"].wait(gp, ("last", pt, pe))
                            sm[f"s_ws{ec % 2}"].inc(
                                gp.dma_start(
                                    out=wstat[ec % 2][:],
                                    in_=wmix[ec * 128:(ec + 1) * 128, :]),
                                key=("ws", t, ec), q=16)
                            for _ in range(6):
                                prep_one()
                    while pi_box[0] < NPREP:
                        prep_one()

                @blk.tensor
                def _(te):
                    sm["s_const"].wait(te, "consts")
                    sm["s_vinit"].wait(te, "vinit")
                    bankP = {0: None, 1: None, 2: None}

                    def emit_stats(te_, t_, ec_):
                        if ec_ == 0 and t_ > 0:
                            sm["s_rv"].wait(te_, ("rv", t_ - 1))
                        sm["s_r"].wait(te_, ("r", t_, ec_))
                        sm["s_sq"].wait(te_, ("sq", t_, ec_))
                        te_.matmul(psSm[:], onesc_t[:], r_t3[:, ec_, :],
                                   start=(ec_ == 0), stop=(ec_ == EC - 1))
                        sm["s_st"].inc(
                            te_.matmul(psSq[:], onesc_t[:], spA[ec_ % 2][:, 0:512],
                                       start=(ec_ == 0), stop=(ec_ == EC - 1)),
                            key=("st", t_, ec_))

                    for t, (u0, St, wm) in enumerate(TILES):
                        rngs = _rngs(wm)
                        sm["s_ubf"].wait(te, ("ubf", t))
                        for ec in range(EC):
                            if ec >= 2:
                                emit_stats(te, t, ec - 2)
                            sm[f"s_ws{ec % 2}"].wait(te, ("ws", t, ec))
                            for ri, (off, W_) in enumerate(rngs):
                                for j in range(3):
                                    if bankP[j] is not None:
                                        s, k2 = bankP[j]
                                        sm[s].wait(te, k2)
                                    mm = None
                                    for dc in range(EC):
                                        mm = te.matmul(
                                            psP[j][:, 0:W_],
                                            wstat4[ec % 2][:, j, dc, :],
                                            u_bf3[:, dc, off:off + W_],
                                            start=(dc == 0), stop=(dc == EC - 1))
                                    sm["s_prj"].inc(mm, key=("prj", t, ec, ri, j))
                                    last_ri = ri == len(rngs) - 1
                                    bankP[j] = {
                                        0: ("s_sp", ("sp", t, ec, ri)),
                                        1: ("s_w", ("w", t, ec, ri)),
                                        2: (("s_cE", ("cE", t, ec)) if last_ri
                                            else ("s_prj", ("prj", t, ec, ri, 2))),
                                    }[j]
                            sm["s_prj"].mark(("last", t, ec))
                        emit_stats(te, t, EC - 2)
                        emit_stats(te, t, EC - 1)
                        sm["s_rv"].wait(te, ("rv", t))
                        sm["s_rs"].wait(te, ("rs", t))
                        if t > 0:
                            sm["s_x1"].wait(te, ("x1", t - 1, EC - 1))
                        te.matmul(psX[:], onesr_t[:], mrow[:], start=True, stop=True)
                        sm["s_bc"].inc(
                            te.matmul(psIV[:], onesr_t[:], vrow[:],
                                      start=True, stop=True),
                            key=("bc", t))

                @blk.scalar
                def _(se):
                    ss = sm["s_sself"]

                    def stag(instr):
                        ss.inc(instr)
                        return ss.n

                    def swait(v):
                        se.wait_ge(ss.h, v)

                    def stag_noinstr():
                        ss.n += 1
                        return ss.n

                    sm["s_const"].wait(se, "consts")
                    psub_box = [0]

                    def prep_sub():
                        pi = psub_box[0]
                        if pi >= NPREP:
                            return
                        mat, _ = PREP_CHUNKS[pi]
                        w_ = II if mat in ("g", "u") else D
                        sm[f"s_pi{pi % 2}"].wait(se, ("pi", pi))
                        sm["s_ps"].inc(
                            se.activation(out=pstg[pi % 2][:, 0:w_],
                                          in_=pstg[pi % 2][:, 0:w_],
                                          func=AF.Identity, bias=-128.0),
                            key=("sub", pi))
                        psub_box[0] += 1

                    v = stag(se.activation(out=spl_t[:], in_=araw_t[:], func=AF.Abs))
                    swait(v)
                    v = stag(se.activation(out=A_t[:], in_=spl_t[:], func=AF.Exp,
                                           scale=-1.0))
                    swait(v)
                    v = stag(se.activation(out=spl_t[:], in_=A_t[:], func=AF.Ln,
                                           bias=1.0))
                    swait(v)
                    sm["s_sinit"].inc(
                        se.activation(out=A_t[:], in_=araw_t[:], func=AF.Relu),
                        key="sinit")

                    def sp_blk(t, ec):
                        rngs = _rngs(TILES[t][2])
                        ge = t * EC + ec
                        for ri, (off, W_) in enumerate(rngs):
                            sm["s_prj"].wait(se, ("prj", t, ec, ri, 0))
                            if ge >= 2:
                                pt, pe = divmod(ge - 2, EC)
                                sm["s_st"].wait(se, ("st", pt, pe))
                            bcol = bdt_t[:, ec:ec + 1]
                            v = stag(se.activation(
                                out=spA[ec % 2][:, off:off + W_],
                                in_=psP[0][:, 0:W_], func=AF.Exp, bias=bcol))
                            swait(v)
                            sm["s_sp"].inc(
                                se.activation(out=spA[ec % 2][:, off:off + W_],
                                              in_=spA[ec % 2][:, off:off + W_],
                                              func=AF.Ln, bias=1.0),
                                key=("sp", t, ec, ri))

                    def ce_blk(t, ec):
                        rngs = _rngs(TILES[t][2])
                        ge = t * EC + ec
                        sm["s_prj"].wait(se, ("prj", t, ec, len(rngs) - 1, 2))
                        if ge >= 2:
                            pt, pe = divmod(ge - 2, EC)
                            sm["s_r"].wait(se, ("r", pt, pe))
                        sm["s_cE"].inc(
                            se.activation(out=c_b[ec % 2][:],
                                          in_=psP[2][:, 0:512],
                                          func=AF.Identity,
                                          bias=bc_t[:, ec:ec + 1]),
                            key=("cE", t, ec))

                    for t, (u0, St, wm) in enumerate(TILES):
                        rngs = _rngs(wm)
                        for ec in range(EC):
                            ge = t * EC + ec
                            for _ in range(6):
                                prep_sub()
                            if ec == 0:
                                sp_blk(t, 0)
                            ce_blk(t, ec)
                            if ec + 1 < EC:
                                sp_blk(t, ec + 1)
                            sm["s_vinit"].wait(se, "vinit")
                            sm["s_w"].wait(se, ("w", t, ec, len(rngs) - 1))
                            for n in range(N):
                                gn = ge * N + n
                                if gn >= 4:
                                    pt2, rest = divmod(gn - 4, EC * N)
                                    pe2, pn2 = divmod(rest, N)
                                    sm["s_scan"].wait(se, ("scan", pt2, pe2, pn2))
                                sm["s_ab"].inc(
                                    se.activation(
                                        out=ab_b[n % 4][:, 0:St],
                                        in_=dt_b[ec % 2][:, 0:St], func=AF.Exp,
                                        scale=A_t[:, ec * N + n:ec * N + n + 1]),
                                    key=("ab", t, ec, n))
                            sm["s_r"].wait(se, ("r", t, ec))
                            sm["s_sq"].inc(
                                se.activation(out=spA[ec % 2][:, 0:512],
                                              in_=r_t3[:, ec, :], func=AF.Square),
                                key=("sq", t, ec))
                            tm = t * 512
                            if stage == 1:
                                sm[f"s_dbg{ec % 2}"].inc(
                                    se.dma_start(
                                        out=out[ec * 128:(ec + 1) * 128,
                                                tm:tm + 512],
                                        in_=r_t3[:, ec, :]),
                                    key=("dbg", t, ec), q=16)
                        sm["s_rv"].wait(se, ("rv", t))
                        v = stag(se.activation(out=vrow[:], in_=vrow[:], func=AF.Ln,
                                               bias=1e-5))
                        swait(v)
                        sm["s_rs"].inc(
                            se.activation(out=vrow[:], in_=vrow[:], func=AF.Exp,
                                          scale=-0.5),
                            key=("rs", t))
                        if stage >= 2:
                            tm = t * 512
                            for ec in range(EC):
                                ge = t * EC + ec
                                sm["s_x1"].wait(se, ("x1", t, ec))
                                if ge >= 2:
                                    pt, pe = divmod(ge - 2, EC)
                                    sm[f"s_x1w{(ge - 2) % 4}"].wait(se,
                                                                   ("x1w", pt, pe))
                                cast_i = se.activation(out=x1c_b[ec % 2][:],
                                                       in_=x1_b[ec % 2][:],
                                                       func=AF.Copy)
                                sm["s_x1b"].inc(cast_i, key=("x1b", t, ec))
                                se.wait_ge(sm["s_x1b"].h, sm["s_x1b"].n)
                                sm[f"s_x1w{ge % 4}"].inc(
                                    se.dma_start(
                                        out=x1f[ec * 128:(ec + 1) * 128,
                                                tm:tm + 512],
                                        in_=x1_b[ec % 2][:]), q=16)
                                sm[f"s_x1w{ge % 4}"].inc(
                                    se.dma_start(
                                        out=x1bd[ec * 128:(ec + 1) * 128,
                                                 tm:tm + 512],
                                        in_=x1c_b[ec % 2][:]),
                                    key=("x1w", t, ec), q=16)
                                if stage == 2:
                                    sm[f"s_dbg{ec % 2}"].inc(
                                        se.dma_start(
                                            out=out[ec * 128:(ec + 1) * 128,
                                                    tm:tm + 512],
                                            in_=x1_b[ec % 2][:]),
                                        key=("dbg", t, ec), q=16)
                        if stage < 3:
                            sm["s_dbg0"].mark(("dbgt", t))
                            sm["s_dbg1"].mark(("dbgt", t))
                    while psub_box[0] < NPREP:
                        prep_sub()
                    if stage >= 3:
                        for sl in range(4):
                            sm[f"s_x1w{sl}"].mark("fin")
                    if stage < 3:
                        sm["s_dbg0"].wait(se, ("dbgt", NT - 1))
                        sm["s_dbg1"].wait(se, ("dbgt", NT - 1))

                @blk.vector
                def _(ve):
                    vs = sm["s_vself"]

                    def vtag(instr):
                        vs.inc(instr)
                        return vs.n

                    def vwait(v):
                        ve.wait_ge(vs.h, v)

                    sm["s_const"].wait(ve, "consts")
                    sm["s_sinit"].wait(ve, "sinit")
                    v = vtag(ve.tensor_add(A_t[:], A_t[:], spl_t[:]))
                    vwait(v)
                    vtag(ve.tensor_scalar_mul(A_t[:], A_t[:], -1.0))
                    vtag(ve.tensor_mul(cbb_t[:], cbb_t[:], cb2_t[:]))
                    vtag(ve.tensor_scalar_add(dp1_t[:], dp1_t[:], 1.0))
                    vtag(ve.memset(onesc_t[:], 1.0))
                    vwait(vs.n)
                    sm["s_vinit"].inc(ve.memset(onesr_t[:], 1.0), key="vinit")
                    sm["s_vinit"].wait(ve, "vinit")
                    for t, (u0, St, wm) in enumerate(TILES):
                        rngs = _rngs(wm)
                        if t > 0:
                            vwait(vs.n)
                            if stage < 3:
                                sm["s_dbg0"].wait(ve, ("dbgt", t - 1))
                                sm["s_dbg1"].wait(ve, ("dbgt", t - 1))
                        for ec in range(EC):
                            ge = t * EC + ec
                            for ri, (off, W_) in enumerate(rngs):
                                sm["s_sp"].wait(ve, ("sp", t, ec, ri))
                                if ri == 0 and ge >= 2:
                                    pt, pe = divmod(ge - 2, EC)
                                    sm["s_ab"].wait(ve, ("ab", pt, pe, N - 1))
                                v = vtag(ve.tensor_scalar(
                                    out=dt_b[ec % 2][:, off:off + W_],
                                    in0=spA[ec % 2][:, off:off + W_],
                                    scalar1=eps4_t[:, 1:2], scalar2=None,
                                    op0=ALU.add))
                                sm["s_prj"].wait(ve, ("prj", t, ec, ri, 1))
                                vwait(v)
                                v = vtag(ve.scalar_tensor_tensor(
                                    out=w_b[ec % 2][:, off:off + W_],
                                    in0=psP[1][:, 0:W_], scalar=bb_t[:, ec:ec + 1],
                                    in1=dt_b[ec % 2][:, off:off + W_],
                                    op0=ALU.add, op1=ALU.mult))
                                sm[f"s_uch{ec % 2}"].wait(ve, ("uch", t, ec))
                                vwait(v)
                                sm["s_w"].inc(
                                    ve.tensor_mul(w_b[ec % 2][:, off:off + W_],
                                                  w_b[ec % 2][:, off:off + W_],
                                                  u_ch[ec % 2][:, off:off + W_]),
                                    key=("w", t, ec, ri))
                            M0 = wm
                            last = None
                            for n in range(N):
                                sm["s_ab"].wait(ve, ("ab", t, ec, n))
                                init = (0.0 if t == 0
                                        else carry[:, ec * N + n:ec * N + n + 1])
                                sci = ve.tensor_tensor_scan(
                                    out=xp_b[n % 2][:, 0:St],
                                    data0=ab_b[n % 4][:, 0:St],
                                    data1=w_b[ec % 2][:, 0:St],
                                    initial=init, op0=ALU.mult, op1=ALU.add)
                                sm["s_scan"].inc(sci, key=("scan", t, ec, n))
                                ve.wait_ge(sm["s_scan"].h, sm["s_scan"].n)
                                if n > 0:
                                    vwait(vs.n)
                                vtag(ve.tensor_copy(
                                    carry[:, ec * N + n:ec * N + n + 1],
                                    xp_b[n % 2][:, St - 1:St]))
                                if n == 0:
                                    last = ve.tensor_scalar(
                                        out=ya_b[ec % 2][:],
                                        in0=xp_b[0][:, M0:M0 + 512],
                                        scalar1=cbb_t[:, ec * N:ec * N + 1],
                                        scalar2=None, op0=ALU.mult)
                                else:
                                    last = ve.scalar_tensor_tensor(
                                        out=ya_b[ec % 2][:],
                                        in0=xp_b[n % 2][:, M0:M0 + 512],
                                        scalar=cbb_t[:, ec * N + n:ec * N + n + 1],
                                        in1=ya_b[ec % 2][:], op0=ALU.mult,
                                        op1=ALU.add)
                                vtag(last)

                            sm["s_cE"].wait(ve, ("cE", t, ec))
                            vwait(vs.n)
                            v = vtag(ve.tensor_mul(ya_b[ec % 2][:], ya_b[ec % 2][:],
                                                   c_b[ec % 2][:]))
                            if t > 0:
                                sm["s_st"].wait(ve, ("st", t - 1, ec))
                                sm["s_sq"].wait(ve, ("sq", t - 1, ec))
                            vwait(v)
                            sm["s_r"].inc(
                                ve.scalar_tensor_tensor(
                                    out=r_t3[:, ec, :],
                                    in0=u_ch[ec % 2][:, M0:M0 + 512],
                                    scalar=dp1_t[:, ec:ec + 1],
                                    in1=ya_b[ec % 2][:], op0=ALU.mult, op1=ALU.add),
                                key=("r", t, ec))
                        sm["s_st"].wait(ve, ("st", t, EC - 1))
                        v = vtag(ve.tensor_scalar(out=mrow[:], in0=psSm[:],
                                                  scalar1=1.0 / D,
                                                  scalar2=None, op0=ALU.mult))
                        vwait(v)
                        v = vtag(ve.tensor_mul(vrow[:], mrow[:], mrow[:]))
                        vwait(v)
                        sm["s_rv"].inc(
                            ve.scalar_tensor_tensor(out=vrow[:], in0=psSq[:],
                                                    scalar=1.0 / D, in1=vrow[:],
                                                    op0=ALU.mult, op1=ALU.subtract),
                            key=("rv", t))
                        if stage >= 2:
                            sm["s_bc"].wait(ve, ("bc", t))
                            for ec in range(EC):
                                ge = t * EC + ec
                                if ge >= 2:
                                    pt, pe = divmod(ge - 2, EC)
                                    sm["s_x1b"].wait(ve, ("x1b", pt, pe))
                                    sm[f"s_x1w{(ge - 2) % 4}"].wait(ve, ("x1w", pt, pe))
                                    if stage == 2:
                                        sm[f"s_dbg{pe % 2}"].wait(ve, ("dbg", pt, pe))
                                v = vtag(ve.tensor_sub(x1_b[ec % 2][:],
                                                       r_t3[:, ec, :], psX[:]))
                                vwait(v)
                                sm["s_x1"].inc(
                                    ve.tensor_mul(x1_b[ec % 2][:], x1_b[ec % 2][:],
                                                  psIV[:]),
                                    key=("x1", t, ec))
                        else:
                            for ec in range(EC):
                                sm["s_x1"].inc(ve.nop(), key=("x1", t, ec))

        if stage >= 3:
            with ExitStack() as cf:
                fsb = lambda nm, sh, dt: cf.enter_context(nc.sbuf_tensor(nm, sh, dt))
                fps = lambda nm, sh: cf.enter_context(nc.psum_tensor(nm, sh, F32))
                x1n = fsb("x1n", [128, EC * 512], BF16)
                x1n3 = x1n[:].rearrange("p (k t) -> p k t", k=EC)
                gstat = [fsb(f"gstat{i}", [128, EC * 2 * 128], BF16) for i in range(2)]
                gstat4 = [g[:].rearrange("p (k i e) -> p k i e", k=EC, i=2)
                          for g in gstat]
                ustat = [fsb(f"ustat{i}", [128, EC * 2 * 128], BF16) for i in range(2)]
                ustat4 = [u_[:].rearrange("p (k i e) -> p k i e", k=EC, i=2)
                          for u_ in ustat]
                h_t = fsb("h_t", [128, IC * 512], BF16)
                h_t3 = h_t[:].rearrange("p (k t) -> p k t", k=IC)
                dstat = [fsb(f"dstat{i}", [128, IC * 128], BF16) for i in range(2)]
                dstat3 = [d[:].rearrange("p (k e) -> p k e", k=IC) for d in dstat]
                sg_b = [fsb(f"sg_b{i}", [128, 512], F32) for i in range(2)]
                x1f_ch = [fsb(f"x1f_ch{i}", [128, 512], F32) for i in range(2)]
                r2_b = [fsb(f"r2_b{i}", [128, 512], F32) for i in range(2)]
                sq2_b = [fsb(f"sq2_b{i}", [128, 512], F32) for i in range(2)]
                r2r_b = [fsb(f"r2r_b{i}", [128, 512], F32) for i in range(2)]
                x2_b = [fsb(f"x2_b{i}", [128, 512], F32) for i in range(2)]
                psG = [fps(f"psG{j}", [128, 512]) for j in range(2)]
                psU = fps("psU", [128, 512])
                psD = fps("psD", [128, 512])
                psS2m = fps("psS2m", [1, 512])
                psS2q = fps("psS2q", [1, 512])
                psM2 = fps("psM2", [128, 512])
                psV2 = fps("psV2", [128, 512])

                with nc.Block() as fbk:
                    @fbk.gpsimd
                    def _(gp):
                        for ts in range(4):
                            tm = ts * 512
                            if ts == 0:
                                for sl in range(4):
                                    sm[f"s_x1w{sl}"].wait(gp, "fin")
                                for nmm in ("s_pog0", "s_pog1", "s_pou0",
                                            "s_pou1", "s_pod0", "s_pod1"):
                                    sm[nmm].wait(gp, "fin")
                            else:
                                sm["s_fu"].wait(gp, ("fu", ts - 1, IC - 1))
                            sm["s_fx"].inc(
                                gp.dma_start(
                                    out=x1n3[:, :, :],
                                    in_=x1bd[:, tm:tm + 512]
                                    .rearrange("(k p) t -> p k t", p=128)),
                                key=("fx", ts), q=16)
                            for icg in range(IC // 2):
                                rot = icg % 2
                                gi = ts * (IC // 2) + icg
                                if gi >= 2:
                                    pts, pik = divmod(gi - 2, IC // 2)
                                    sm["s_fu"].wait(gp, ("fu", pts, pik * 2 + 1))
                                i0 = icg * 256
                                sm[f"s_fgs{rot}"].inc(
                                    gp.dma_start(
                                        out=gstat4[rot][:, :, :, :],
                                        in_=gwbT[:, i0:i0 + 256].rearrange(
                                            "(k p) (i e) -> p k i e", p=128, i=2)),
                                    key=("fgsg", ts, icg), q=16)
                                sm[f"s_fus{rot}"].inc(
                                    gp.dma_start(
                                        out=ustat4[rot][:, :, :, :],
                                        in_=uwbT[:, i0:i0 + 256].rearrange(
                                            "(k p) (i e) -> p k i e", p=128, i=2)),
                                    key=("fgs", ts, icg), q=16)

                    @fbk.sync
                    def _(sy):
                        for sl in range(4):
                            sm[f"s_x1w{sl}"].wait(sy, "fin")
                        sm["s_pod0"].wait(sy, "fin")
                        sm["s_pod1"].wait(sy, "fin")
                        for ts in range(4):
                            tm = ts * 512
                            for hc in range(EC):
                                gh = ts * EC + hc
                                if gh >= 2:
                                    pts, phc = divmod(gh - 2, EC)
                                    sm["s_fdm"].wait(sy, ("fdm", pts, phc))
                                sm[f"s_fds{hc % 2}"].inc(
                                    sy.dma_start(
                                        out=dstat3[hc % 2][:, :, :],
                                        in_=dwbT[:, hc * 128:(hc + 1) * 128]
                                        .rearrange("(k p) e -> p k e", p=128)),
                                    key=("fds", ts, hc), q=16)
                                if gh >= 2:
                                    pts, phc = divmod(gh - 2, EC)
                                    sm["s_fr2"].wait(sy, ("fr2", pts, phc))
                                sm[f"s_fx1r{hc % 2}"].inc(
                                    sy.dma_start(
                                        out=x1f_ch[hc % 2][:],
                                        in_=x1f[hc * 128:(hc + 1) * 128,
                                                tm:tm + 512]),
                                    key=("fx1r", ts, hc), q=16)
                                if hc >= 1:
                                    sm["s_fr2"].wait(sy, ("fr2", ts, hc - 1))
                                    sm[f"s_fr2w{(hc - 1) % 2}"].inc(
                                        sy.dma_start(
                                            out=r2f[(hc - 1) * 128:hc * 128,
                                                    tm:tm + 512],
                                            in_=r2_b[(hc - 1) % 2][:]),
                                        key=("fr2w", ts, hc - 1), q=16)
                            sm["s_fr2"].wait(sy, ("fr2", ts, EC - 1))
                            sm[f"s_fr2w{(EC - 1) % 2}"].inc(
                                sy.dma_start(
                                    out=r2f[(EC - 1) * 128:EC * 128, tm:tm + 512],
                                    in_=r2_b[(EC - 1) % 2][:]),
                                key=("fr2w", ts, EC - 1), q=16)
                            sm["s_fr2w0"].mark(("r2wt", ts))
                            sm["s_fr2w1"].mark(("r2wt", ts))
                            for hc in range(EC):
                                gh = ts * EC + hc
                                if gh >= 2:
                                    pts, phc = divmod(gh - 2, EC)
                                    sm["s_fx2"].wait(sy, ("fx2", pts, phc))
                                sm["s_fr2w0"].wait(sy, ("r2wt", ts))
                                sm["s_fr2w1"].wait(sy, ("r2wt", ts))
                                sm[f"s_fr2r{hc % 2}"].inc(
                                    sy.dma_start(
                                        out=r2r_b[hc % 2][:],
                                        in_=r2f[hc * 128:(hc + 1) * 128,
                                                tm:tm + 512]),
                                    key=("fr2r", ts, hc), q=16)
                                sm["s_fx2"].wait(sy, ("fx2", ts, hc))
                                sm[f"s_fo{hc % 2}"].inc(
                                    sy.dma_start(
                                        out=out[hc * 128:(hc + 1) * 128,
                                                tm:tm + 512],
                                        in_=x2_b[hc % 2][:]),
                                    key=("fo", ts, hc), q=16)
                        sy.wait_ge(sm["s_fo0"].h, sm["s_fo0"].n)
                        sy.wait_ge(sm["s_fo1"].h, sm["s_fo1"].n)

                    @fbk.tensor
                    def _(te):
                        def stats2(ts_, hc_):
                            if hc_ == 0 and ts_ > 0:
                                sm["s_frv"].wait(te, ("frv", ts_ - 1))
                            sm["s_fr2"].wait(te, ("fr2", ts_, hc_))
                            sm["s_fsq"].wait(te, ("fsq", ts_, hc_))
                            te.matmul(psS2m[:], onesc_t[:], r2_b[hc_ % 2][:],
                                      start=(hc_ == 0), stop=(hc_ == EC - 1))
                            sm["s_fst"].inc(
                                te.matmul(psS2q[:], onesc_t[:], sq2_b[hc_ % 2][:],
                                          start=(hc_ == 0), stop=(hc_ == EC - 1)),
                                key=("fst", ts_, hc_))

                        for ts in range(4):
                            sm["s_fx"].wait(te, ("fx", ts))
                            for icg in range(IC // 2):
                                rot = icg % 2
                                sm[f"s_fgs{rot}"].wait(te, ("fgsg", ts, icg))
                                sm[f"s_fus{rot}"].wait(te, ("fgs", ts, icg))
                                for ii in range(2):
                                    ic = icg * 2 + ii
                                    gi2 = ts * IC + ic
                                    if gi2 >= 2:
                                        pts, pic = divmod(gi2 - 2, IC)
                                        sm["s_fsg"].wait(te, ("fsg", pts, pic))
                                    mm = None
                                    for k in range(EC):
                                        mm = te.matmul(
                                            psG[ic % 2][:],
                                            gstat4[rot][:, k, ii, :],
                                            x1n3[:, k, :],
                                            start=(k == 0), stop=(k == EC - 1))
                                    sm["s_fg"].inc(mm, key=("fg", ts, ic))
                                    if gi2 >= 1:
                                        pts, pic = divmod(gi2 - 1, IC)
                                        sm["s_fh"].wait(te, ("fh", pts, pic))
                                    mm = None
                                    for k in range(EC):
                                        mm = te.matmul(
                                            psU[:],
                                            ustat4[rot][:, k, ii, :],
                                            x1n3[:, k, :],
                                            start=(k == 0), stop=(k == EC - 1))
                                    sm["s_fu"].inc(mm, key=("fu", ts, ic))
                            for hc in range(EC):
                                if hc >= 2:
                                    stats2(ts, hc - 2)
                                sm[f"s_fds{hc % 2}"].wait(te, ("fds", ts, hc))
                                sm["s_fh"].wait(te, ("fh", ts, IC - 1))
                                if hc >= 1:
                                    sm["s_fr2"].wait(te, ("fr2", ts, hc - 1))
                                elif ts > 0:
                                    sm["s_fr2"].wait(te, ("fr2", ts - 1, EC - 1))
                                mm = None
                                for k in range(IC):
                                    mm = te.matmul(
                                        psD[:], dstat3[hc % 2][:, k, :],
                                        h_t3[:, k, :],
                                        start=(k == 0), stop=(k == IC - 1))
                                sm["s_fdm"].inc(mm, key=("fdm", ts, hc))
                            stats2(ts, EC - 2)
                            stats2(ts, EC - 1)
                            sm["s_frv"].wait(te, ("frv", ts))
                            sm["s_frs"].wait(te, ("frs", ts))
                            if ts > 0:
                                sm["s_fx2"].wait(te, ("fx2", ts - 1, EC - 1))
                            te.matmul(psM2[:], onesr_t[:], mrow[:],
                                      start=True, stop=True)
                            sm["s_fbc"].inc(
                                te.matmul(psV2[:], onesr_t[:], vrow[:],
                                          start=True, stop=True),
                                key=("fbc", ts))

                    @fbk.scalar
                    def _(se):
                        ss = sm["s_sself"]
                        for nmm in ("s_pog0", "s_pog1", "s_pou0", "s_pou1",
                                    "s_pod0", "s_pod1"):
                            sm[nmm].wait(se, "fin")
                        for ts in range(4):
                            for ic in range(IC):
                                gi2 = ts * IC + ic
                                sm["s_fg"].wait(se, ("fg", ts, ic))
                                if gi2 >= 2:
                                    pts, pic = divmod(gi2 - 2, IC)
                                    sm["s_fh"].wait(se, ("fh", pts, pic))
                                sm["s_fsg"].inc(
                                    se.activation(out=sg_b[ic % 2][:],
                                                  in_=psG[ic % 2][:],
                                                  func=AF.Silu,
                                                  scale=gs_t[:, ic:ic + 1]),
                                    key=("fsg", ts, ic))
                            for hc in range(EC):
                                gh = ts * EC + hc
                                sm["s_fr2"].wait(se, ("fr2", ts, hc))
                                if gh >= 2:
                                    pts, phc = divmod(gh - 2, EC)
                                    sm["s_fst"].wait(se, ("fst", pts, phc))
                                sm["s_fsq"].inc(
                                    se.activation(out=sq2_b[hc % 2][:],
                                                  in_=r2_b[hc % 2][:],
                                                  func=AF.Square),
                                    key=("fsq", ts, hc))
                            sm["s_frv"].wait(se, ("frv", ts))
                            ss.inc(se.activation(out=vrow[:], in_=vrow[:],
                                                 func=AF.Ln, bias=1e-5))
                            se.wait_ge(ss.h, ss.n)
                            sm["s_frs"].inc(
                                se.activation(out=vrow[:], in_=vrow[:],
                                              func=AF.Exp, scale=-0.5),
                                key=("frs", ts))

                    @fbk.vector
                    def _(ve):
                        vs = sm["s_vself"]
                        for nmm in ("s_pog0", "s_pog1", "s_pou0", "s_pou1",
                                    "s_pod0", "s_pod1"):
                            sm[nmm].wait(ve, "fin")
                        for ts in range(4):
                            for ic in range(IC):
                                gi2 = ts * IC + ic
                                sm["s_fu"].wait(ve, ("fu", ts, ic))
                                sm["s_fsg"].wait(ve, ("fsg", ts, ic))
                                if ts > 0 and ic == 0:
                                    sm["s_fdm"].wait(ve, ("fdm", ts - 1, EC - 1))
                                sm["s_fh"].inc(
                                    ve.scalar_tensor_tensor(
                                        out=h_t3[:, ic, :], in0=psU[:],
                                        scalar=us_t[:, ic:ic + 1],
                                        in1=sg_b[ic % 2][:],
                                        op0=ALU.mult, op1=ALU.mult),
                                    key=("fh", ts, ic))
                            for hc in range(EC):
                                gh = ts * EC + hc
                                sm["s_fdm"].wait(ve, ("fdm", ts, hc))
                                sm[f"s_fx1r{hc % 2}"].wait(ve, ("fx1r", ts, hc))
                                if gh >= 2:
                                    pts, phc = divmod(gh - 2, EC)
                                    sm[f"s_fr2w{phc % 2}"].wait(ve,
                                                                ("fr2w", pts, phc))
                                    sm["s_fst"].wait(ve, ("fst", pts, phc))
                                sm["s_fr2"].inc(
                                    ve.scalar_tensor_tensor(
                                        out=r2_b[hc % 2][:], in0=psD[:],
                                        scalar=ds_t[:, hc:hc + 1],
                                        in1=x1f_ch[hc % 2][:],
                                        op0=ALU.mult, op1=ALU.add),
                                    key=("fr2", ts, hc))
                            sm["s_fst"].wait(ve, ("fst", ts, EC - 1))
                            v = vs.n + 1
                            vs.inc(ve.tensor_scalar(out=mrow[:], in0=psS2m[:],
                                                    scalar1=1.0 / D, scalar2=None,
                                                    op0=ALU.mult))
                            ve.wait_ge(vs.h, v)
                            v = vs.n + 1
                            vs.inc(ve.tensor_mul(vrow[:], mrow[:], mrow[:]))
                            ve.wait_ge(vs.h, v)
                            sm["s_frv"].inc(
                                ve.scalar_tensor_tensor(out=vrow[:], in0=psS2q[:],
                                                        scalar=1.0 / D, in1=vrow[:],
                                                        op0=ALU.mult,
                                                        op1=ALU.subtract),
                                key=("frv", ts))
                            sm["s_fbc"].wait(ve, ("fbc", ts))
                            for hc in range(EC):
                                gh = ts * EC + hc
                                sm[f"s_fr2r{hc % 2}"].wait(ve, ("fr2r", ts, hc))
                                if gh >= 2:
                                    pts, phc = divmod(gh - 2, EC)
                                    sm[f"s_fo{phc % 2}"].wait(ve, ("fo", pts, phc))
                                v = vs.n + 1
                                vs.inc(ve.tensor_sub(x2_b[hc % 2][:],
                                                     r2r_b[hc % 2][:], psM2[:]))
                                ve.wait_ge(vs.h, v)
                                sm["s_fx2"].inc(
                                    ve.tensor_mul(x2_b[hc % 2][:], x2_b[hc % 2][:],
                                                  psV2[:]),
                                    key=("fx2", ts, hc))

        tables = {name: dict(s.keys) for name, s in sm.items()}
    return nc, tables


def build(stage=3):
    _, tables = build_nc(stage=stage, table=None)
    nc, t2 = build_nc(stage=stage, table=tables)
    assert t2 == tables, "nondeterministic build"
    return nc, tables


def make_inmaps(inputs):
    u = np.asarray(inputs["L0_attn_u"], np.float32)

    def col(v):
        return np.ascontiguousarray(np.asarray(v, np.float32).reshape(EC, 128).T)

    def colN(v):
        v = np.asarray(v, np.float32).reshape(EC, 128, N)
        return np.ascontiguousarray(v.transpose(1, 0, 2).reshape(128, EC * N))

    wmix_arr = np.stack([np.asarray(inputs["L0_attn_Wdt"], np.float32),
                         np.asarray(inputs["L0_attn_Wb"], np.float32),
                         np.asarray(inputs["L0_attn_Wc"], np.float32)])
    wmix_arr = np.ascontiguousarray(
        wmix_arr.reshape(3, EC, 128, EC, 128).transpose(3, 2, 0, 1, 4)
        .reshape(D, 3 * EC * 128))
    com = {
        "wmix": wmix_arr,
        "bdt": col(inputs["L0_attn_bdt"]), "bb": col(inputs["L0_attn_bb"]),
        "bc": col(inputs["L0_attn_bc"]), "araw": colN(inputs["L0_attn_a_raw"]),
        "bbase": colN(inputs["L0_attn_B_base"]),
        "cbase": colN(inputs["L0_attn_C_base"]),
        "dvec": col(inputs["L0_attn_D"]),
        "cin": np.zeros((1, 8), np.float32),
        "gw": np.ascontiguousarray(np.asarray(inputs["L0_ffn_gate_w"], np.int32).T),
        "uw": np.ascontiguousarray(np.asarray(inputs["L0_ffn_up_w"], np.int32).T),
        "dw": np.ascontiguousarray(np.asarray(inputs["L0_ffn_down_w"], np.int32).T),
        "gs": np.ascontiguousarray(
            np.asarray(inputs["L0_ffn_gate_scale"], np.float32).reshape(IC, 128).T),
        "us": np.ascontiguousarray(
            np.asarray(inputs["L0_ffn_up_scale"], np.float32).reshape(IC, 128).T),
        "ds": np.ascontiguousarray(
            np.asarray(inputs["L0_ffn_down_scale"], np.float32).reshape(EC, 128).T),
        "ccol": np.ascontiguousarray(
            np.tile(np.array([[-128.0, 1e-4]], np.float32), (128, 1))),
    }
    maps = []
    for c in range(8):
        b, half = c // 2, c % 2
        useg = np.zeros((WARM + TOK, D), np.float32)
        start = half * TOK
        useg[WARM:] = u[b, start:start + TOK]
        if half == 1:
            useg[:WARM] = u[b, start - WARM:start]
        maps.append({"u": np.ascontiguousarray(useg.T), **com})
    return maps


def kernel(**inputs):
    from concourse.bass_utils import run_bass_kernel_spmd

    maps = make_inmaps(inputs)
    nc, _tables = build(stage=3)
    res = run_bass_kernel_spmd(nc, maps, core_ids=list(range(8)))
    out = np.empty((4, 4096, 2048), np.float32)
    for c in range(8):
        b, half = c // 2, c % 2
        got = np.asarray(res.results[c]["out"])
        out[b, half * TOK:(half + 1) * TOK, :] = got.T
    return out


# version 07
# speedup vs baseline: 1.0314x; 1.0314x over previous
import numpy as np
from contextlib import ExitStack

import concourse.bass as bass
import concourse.mybir as mybir

F32 = mybir.dt.float32
BF16 = mybir.dt.bfloat16
I32 = mybir.dt.int32
AF = mybir.ActivationFunctionType
ALU = mybir.AluOpType

D, N, II = 2048, 16, 8192
WARM, TOK = 256, 2048
EC = D // 128
IC = II // 128
TILES = [(0, 768, 256), (768, 512, 0), (1280, 512, 0), (1792, 512, 0)]
NT = len(TILES)


def _rngs(wm):
    return [(0, wm), (wm, 512)] if wm else [(0, 512)]


class Sem:
    def __init__(self, nc, ctx, name, table):
        self.h = ctx.enter_context(nc.semaphore(name))
        self.name, self.n, self.keys, self.table = name, 0, {}, table

    def inc(self, instr, key=None, q=1):
        self.n += q
        if key is not None:
            assert key not in self.keys, (self.name, key)
            self.keys[key] = self.n
        instr.then_inc(self.h, q)

    def mark(self, key):
        self.keys[key] = self.n

    def at(self, key):
        src = self.table[self.name] if self.table is not None else self.keys
        return src.get(key)

    def wait(self, eng, key, required=True):
        v = self.at(key)
        if v is None:
            if required and self.table is not None:
                raise KeyError(f"sem {self.name}: missing key {key}")
            return
        eng.wait_ge(self.h, v)


SEM_NAMES = [
    "s_const", "s_uch0", "s_uch1", "s_ubf", "s_ws0", "s_ws1",
    "s_pi0", "s_pi1", "s_ps", "s_pog0", "s_pog1", "s_pou0", "s_pou1", "s_pod0", "s_pod1",
    "s_sinit", "s_vinit",
    "s_prj", "s_sp", "s_w", "s_cE", "s_ab", "s_scan", "s_y", "s_r",
    "s_sq", "s_st", "s_rv", "s_rs", "s_bc", "s_x1", "s_x1b", "s_x1w0", "s_x1w1", "s_x1w2", "s_x1w3", "s_dbg0", "s_dbg1",
    "s_fx", "s_fgs0", "s_fgs1", "s_fus0", "s_fus1", "s_fds0", "s_fds1", "s_fg", "s_fu", "s_fsg", "s_fh",
    "s_fdm", "s_fr2", "s_fr2w0", "s_fr2w1", "s_fx1r0", "s_fx1r1",
    "s_fr2r0", "s_fr2r1", "s_fsq", "s_fst",
    "s_frv", "s_frs", "s_fbc", "s_fx2", "s_fo0", "s_fo1",
    "s_vself", "s_sself",
]

PREP_CHUNKS = ([("g", i) for i in range(EC)] + [("u", i) for i in range(EC)]
               + [("d", i) for i in range(IC)])
NPREP = len(PREP_CHUNKS)


def _po_sem(pi):
    return f"s_po{PREP_CHUNKS[pi][0]}{pi % 2}"
NCONSTD = 12


def build_nc(stage=3, table=None):
    nc = bass.Bass("TRN2", target_bir_lowering=False)

    u_in = nc.dram_tensor("u", [D, WARM + TOK], F32, kind="ExternalInput").ap()
    wmix = nc.dram_tensor("wmix", [D, 3 * EC * 128], F32, kind="ExternalInput").ap()
    bdt = nc.dram_tensor("bdt", [128, EC], F32, kind="ExternalInput").ap()
    bbi = nc.dram_tensor("bb", [128, EC], F32, kind="ExternalInput").ap()
    bci = nc.dram_tensor("bc", [128, EC], F32, kind="ExternalInput").ap()
    araw = nc.dram_tensor("araw", [128, EC * N], F32, kind="ExternalInput").ap()
    bbase = nc.dram_tensor("bbase", [128, EC * N], F32, kind="ExternalInput").ap()
    cbase = nc.dram_tensor("cbase", [128, EC * N], F32, kind="ExternalInput").ap()
    dvec = nc.dram_tensor("dvec", [128, EC], F32, kind="ExternalInput").ap()
    cin = nc.dram_tensor("cin", [1, 8], F32, kind="ExternalInput").ap()
    ccol = nc.dram_tensor("ccol", [128, 2], F32, kind="ExternalInput").ap()
    gw = nc.dram_tensor("gw", [D, II], I32, kind="ExternalInput").ap()
    uwq = nc.dram_tensor("uw", [D, II], I32, kind="ExternalInput").ap()
    dwq = nc.dram_tensor("dw", [II, D], I32, kind="ExternalInput").ap()
    gsi = nc.dram_tensor("gs", [128, IC], F32, kind="ExternalInput").ap()
    usi = nc.dram_tensor("us", [128, IC], F32, kind="ExternalInput").ap()
    dsi = nc.dram_tensor("ds", [128, EC], F32, kind="ExternalInput").ap()

    out = nc.dram_tensor("out", [D, TOK], F32, kind="ExternalOutput").ap()

    gwbT = nc.dram_tensor("gwbT", [D, II], BF16, kind="Internal").ap()
    uwbT = nc.dram_tensor("uwbT", [D, II], BF16, kind="Internal").ap()
    dwbT = nc.dram_tensor("dwbT", [II, D], BF16, kind="Internal").ap()
    x1f = nc.dram_tensor("x1f", [D, TOK], F32, kind="Internal").ap()
    x1bd = nc.dram_tensor("x1bd", [D, TOK], BF16, kind="Internal").ap()
    r2f = nc.dram_tensor("r2f", [D, TOK], F32, kind="Internal").ap()

    with ExitStack() as ctx0:
        csb = lambda nm, sh, dt: ctx0.enter_context(nc.sbuf_tensor(nm, sh, dt))
        sm = {name: Sem(nc, ctx0, name, table) for name in SEM_NAMES}

        cst = csb("cst", [1, 8], F32)
        nc.const_aps.register(F32, 1e-5, cst[0:1, 0:1])
        nc.const_aps.register(F32, -0.5, cst[0:1, 1:2])
        ccol_t = csb("ccol_t", [128, 2], F32)
        nc.const_aps.register(F32, -128.0, ccol_t[:, 0:1])
        eps4_t = ccol_t
        bdt_t = csb("bdt_t", [128, EC], F32)
        bb_t = csb("bb_t", [128, EC], F32)
        bc_t = csb("bc_t", [128, EC], F32)
        araw_t = csb("araw_t", [128, EC * N], F32)
        A_t = csb("A_t", [128, EC * N], F32)
        spl_t = csb("spl_t", [128, EC * N], F32)
        cbb_t = csb("cbb_t", [128, EC * N], F32)
        cb2_t = csb("cb2_t", [128, EC * N], F32)
        dp1_t = csb("dp1_t", [128, EC], F32)
        gs_t = csb("gs_t", [128, IC], F32)
        us_t = csb("us_t", [128, IC], F32)
        ds_t = csb("ds_t", [128, EC], F32)
        onesc_t = csb("onesc_t", [128, 1], F32)
        onesr_t = csb("onesr_t", [1, 128], F32)
        carry = csb("carry", [128, EC * N], F32)
        mrow = csb("mrow", [1, 512], F32)
        vrow = csb("vrow", [1, 512], F32)

        with ExitStack() as cm:
            msb = lambda nm, sh, dt: cm.enter_context(nc.sbuf_tensor(nm, sh, dt))
            mps = lambda nm, sh: cm.enter_context(nc.psum_tensor(nm, sh, F32))
            S0 = TILES[0][1]
            u_bf = msb("u_bf", [128, EC * S0], BF16)
            u_bf3 = u_bf[:].rearrange("p (k s) -> p k s", k=EC)
            u_ch = [msb(f"u_ch{i}", [128, S0], F32) for i in range(2)]
            dt_b = [msb(f"dt_b{i}", [128, S0], F32) for i in range(2)]
            w_b = [msb(f"w_b{i}", [128, S0], F32) for i in range(2)]
            c_b = [msb(f"c_b{i}", [128, 512], F32) for i in range(2)]
            spA = [msb(f"spA{i}", [128, S0], F32) for i in range(2)]
            ab_b = [msb(f"ab_b{i}", [128, S0], F32) for i in range(4)]
            xp_b = [msb(f"xp_b{i}", [128, S0], F32) for i in range(2)]
            ya_b = [msb(f"ya_b{i}", [128, 512], F32) for i in range(2)]
            r_t = msb("r_t", [128, EC * 512], F32)
            r_t3 = r_t[:].rearrange("p (k s) -> p k s", k=EC)
            x1_b = [msb(f"x1_b{i}", [128, 512], F32) for i in range(2)]
            x1c_b = [msb(f"x1c_b{i}", [128, 512], BF16) for i in range(2)]
            wstat = [msb(f"wstat{i}", [128, 3 * EC * 128], BF16) for i in range(2)]
            wstat4 = [w[:].rearrange("p (j k e) -> p j k e", j=3, k=EC) for w in wstat]
            pstg = [msb(f"pstg{i}", [128, II], BF16) for i in range(2)]

            psP = [mps(f"psP{j}", [128, 512]) for j in range(3)]
            psSm = mps("psSm", [1, 512])
            psSq = mps("psSq", [1, 512])
            psX = mps("psX", [128, 512])
            psIV = mps("psIV", [128, 512])

            with nc.Block() as blk:
                @blk.sync
                def _(sy):
                    for dst, src in [
                        (cst[:], cin), (bdt_t[:], bdt), (bb_t[:], bbi),
                        (bc_t[:], bci), (araw_t[:], araw), (cbb_t[:], bbase),
                        (cb2_t[:], cbase), (dp1_t[:], dvec), (gs_t[:], gsi),
                        (us_t[:], usi), (ds_t[:], dsi), (ccol_t[:], ccol),
                    ]:
                        sm["s_const"].inc(sy.dma_start(out=dst, in_=src), q=16)
                    sm["s_const"].mark("consts")
                    assert sm["s_const"].n == 16 * NCONSTD
                    po_box = [0]

                    def prep_out():
                        pi = po_box[0]
                        if pi >= NPREP:
                            return
                        mat, blki = PREP_CHUNKS[pi]
                        dst = {"g": gwbT, "u": uwbT, "d": dwbT}[mat]
                        w_ = II if mat in ("g", "u") else D
                        sm["s_ps"].wait(sy, ("sub", pi))
                        sm[_po_sem(pi)].inc(
                            sy.dma_start(out=dst[blki * 128:(blki + 1) * 128, :],
                                         in_=pstg[pi % 2][:, 0:w_]),
                            key=("po", pi), q=16)
                        po_box[0] += 1

                    for t, (u0, St, wm) in enumerate(TILES):
                        for ec in range(EC):
                            ge = t * EC + ec
                            if ge >= 2:
                                pt, pe = divmod(ge - 2, EC)
                                sm["s_r"].wait(sy, ("r", pt, pe))
                            sm[f"s_uch{ec % 2}"].inc(
                                sy.dma_start(
                                    out=u_ch[ec % 2][:, 0:St],
                                    in_=u_in[ec * 128:(ec + 1) * 128, u0:u0 + St]),
                                key=("uch", t, ec), q=16)
                            for _ in range(3):
                                prep_out()
                    while po_box[0] < NPREP:
                        prep_out()
                    for nmm in ("s_pog0", "s_pog1", "s_pou0", "s_pou1",
                                "s_pod0", "s_pod1"):
                        sm[nmm].mark("fin")

                @blk.gpsimd
                def _(gp):
                    pi_box = [0]

                    def prep_one():
                        pi = pi_box[0]
                        if pi >= NPREP:
                            return
                        mat, blki = PREP_CHUNKS[pi]
                        src = {"g": gw, "u": uwq, "d": dwq}[mat]
                        w_ = II if mat in ("g", "u") else D
                        if pi >= 2:
                            sm[_po_sem(pi - 2)].wait(gp, ("po", pi - 2))
                        sm[f"s_pi{pi % 2}"].inc(
                            gp.dma_start(out=pstg[pi % 2][:, 0:w_],
                                         in_=src[blki * 128:(blki + 1) * 128, :]),
                            key=("pi", pi), q=16)
                        pi_box[0] += 1

                    for t, (u0, St, wm) in enumerate(TILES):
                        if t > 0:
                            sm["s_prj"].wait(gp, ("last", t - 1, EC - 1))
                        sm["s_ubf"].inc(
                            gp.dma_start(
                                out=u_bf3[:, :, 0:St],
                                in_=u_in[:, u0:u0 + St]
                                .rearrange("(k p) s -> p k s", p=128)),
                            key=("ubf", t), q=16)
                        for ec in range(EC):
                            ge = t * EC + ec
                            if ge >= 2:
                                pt, pe = divmod(ge - 2, EC)
                                sm["s_prj"].wait(gp, ("last", pt, pe))
                            sm[f"s_ws{ec % 2}"].inc(
                                gp.dma_start(
                                    out=wstat[ec % 2][:],
                                    in_=wmix[ec * 128:(ec + 1) * 128, :]),
                                key=("ws", t, ec), q=16)
                            for _ in range(3):
                                prep_one()
                    while pi_box[0] < NPREP:
                        prep_one()

                @blk.tensor
                def _(te):
                    sm["s_const"].wait(te, "consts")
                    sm["s_vinit"].wait(te, "vinit")
                    bankP = {0: None, 1: None, 2: None}

                    def emit_stats(te_, t_, ec_):
                        if ec_ == 0 and t_ > 0:
                            sm["s_rv"].wait(te_, ("rv", t_ - 1))
                        sm["s_r"].wait(te_, ("r", t_, ec_))
                        sm["s_sq"].wait(te_, ("sq", t_, ec_))
                        te_.matmul(psSm[:], onesc_t[:], r_t3[:, ec_, :],
                                   start=(ec_ == 0), stop=(ec_ == EC - 1))
                        sm["s_st"].inc(
                            te_.matmul(psSq[:], onesc_t[:], spA[ec_ % 2][:, 0:512],
                                       start=(ec_ == 0), stop=(ec_ == EC - 1)),
                            key=("st", t_, ec_))

                    for t, (u0, St, wm) in enumerate(TILES):
                        rngs = _rngs(wm)
                        sm["s_ubf"].wait(te, ("ubf", t))
                        for ec in range(EC):
                            if ec >= 2:
                                emit_stats(te, t, ec - 2)
                            sm[f"s_ws{ec % 2}"].wait(te, ("ws", t, ec))
                            for ri, (off, W_) in enumerate(rngs):
                                for j in range(3):
                                    if bankP[j] is not None:
                                        s, k2 = bankP[j]
                                        sm[s].wait(te, k2)
                                    mm = None
                                    for dc in range(EC):
                                        mm = te.matmul(
                                            psP[j][:, 0:W_],
                                            wstat4[ec % 2][:, j, dc, :],
                                            u_bf3[:, dc, off:off + W_],
                                            start=(dc == 0), stop=(dc == EC - 1))
                                    sm["s_prj"].inc(mm, key=("prj", t, ec, ri, j))
                                    last_ri = ri == len(rngs) - 1
                                    bankP[j] = {
                                        0: ("s_sp", ("sp", t, ec, ri)),
                                        1: ("s_w", ("w", t, ec, ri)),
                                        2: (("s_cE", ("cE", t, ec)) if last_ri
                                            else ("s_prj", ("prj", t, ec, ri, 2))),
                                    }[j]
                            sm["s_prj"].mark(("last", t, ec))
                        emit_stats(te, t, EC - 2)
                        emit_stats(te, t, EC - 1)
                        sm["s_rv"].wait(te, ("rv", t))
                        sm["s_rs"].wait(te, ("rs", t))
                        if t > 0:
                            sm["s_x1"].wait(te, ("x1", t - 1, EC - 1))
                        te.matmul(psX[:], onesr_t[:], mrow[:], start=True, stop=True)
                        sm["s_bc"].inc(
                            te.matmul(psIV[:], onesr_t[:], vrow[:],
                                      start=True, stop=True),
                            key=("bc", t))

                @blk.scalar
                def _(se):
                    ss = sm["s_sself"]

                    def stag(instr):
                        ss.inc(instr)
                        return ss.n

                    def swait(v):
                        se.wait_ge(ss.h, v)

                    def stag_noinstr():
                        ss.n += 1
                        return ss.n

                    sm["s_const"].wait(se, "consts")
                    v = stag(se.activation(out=spl_t[:], in_=araw_t[:], func=AF.Abs))
                    swait(v)
                    v = stag(se.activation(out=A_t[:], in_=spl_t[:], func=AF.Exp,
                                           scale=-1.0))
                    swait(v)
                    v = stag(se.activation(out=spl_t[:], in_=A_t[:], func=AF.Ln,
                                           bias=1.0))
                    swait(v)
                    sm["s_sinit"].inc(
                        se.activation(out=A_t[:], in_=araw_t[:], func=AF.Relu),
                        key="sinit")

                    def sp_blk(t, ec):
                        rngs = _rngs(TILES[t][2])
                        ge = t * EC + ec
                        for ri, (off, W_) in enumerate(rngs):
                            sm["s_prj"].wait(se, ("prj", t, ec, ri, 0))
                            if ge >= 2:
                                pt, pe = divmod(ge - 2, EC)
                                sm["s_st"].wait(se, ("st", pt, pe))
                            bcol = bdt_t[:, ec:ec + 1]
                            v = stag(se.activation(
                                out=spA[ec % 2][:, off:off + W_],
                                in_=psP[0][:, 0:W_], func=AF.Exp, bias=bcol))
                            swait(v)
                            sm["s_sp"].inc(
                                se.activation(out=spA[ec % 2][:, off:off + W_],
                                              in_=spA[ec % 2][:, off:off + W_],
                                              func=AF.Ln, bias=1.0),
                                key=("sp", t, ec, ri))

                    def ce_blk(t, ec):
                        rngs = _rngs(TILES[t][2])
                        ge = t * EC + ec
                        sm["s_prj"].wait(se, ("prj", t, ec, len(rngs) - 1, 2))
                        if ge >= 2:
                            pt, pe = divmod(ge - 2, EC)
                            sm["s_r"].wait(se, ("r", pt, pe))
                        sm["s_cE"].inc(
                            se.activation(out=c_b[ec % 2][:],
                                          in_=psP[2][:, 0:512],
                                          func=AF.Identity,
                                          bias=bc_t[:, ec:ec + 1]),
                            key=("cE", t, ec))

                    for t, (u0, St, wm) in enumerate(TILES):
                        rngs = _rngs(wm)
                        for ec in range(EC):
                            ge = t * EC + ec
                            if ec == 0:
                                sp_blk(t, 0)
                            ce_blk(t, ec)
                            if ec + 1 < EC:
                                sp_blk(t, ec + 1)
                            sm["s_vinit"].wait(se, "vinit")
                            sm["s_w"].wait(se, ("w", t, ec, len(rngs) - 1))
                            for n in range(N):
                                gn = ge * N + n
                                if gn >= 4:
                                    pt2, rest = divmod(gn - 4, EC * N)
                                    pe2, pn2 = divmod(rest, N)
                                    sm["s_scan"].wait(se, ("scan", pt2, pe2, pn2))
                                sm["s_ab"].inc(
                                    se.activation(
                                        out=ab_b[n % 4][:, 0:St],
                                        in_=dt_b[ec % 2][:, 0:St], func=AF.Exp,
                                        scale=A_t[:, ec * N + n:ec * N + n + 1]),
                                    key=("ab", t, ec, n))
                            sm["s_r"].wait(se, ("r", t, ec))
                            sm["s_sq"].inc(
                                se.activation(out=spA[ec % 2][:, 0:512],
                                              in_=r_t3[:, ec, :], func=AF.Square),
                                key=("sq", t, ec))
                            tm = t * 512
                            if stage == 1:
                                sm[f"s_dbg{ec % 2}"].inc(
                                    se.dma_start(
                                        out=out[ec * 128:(ec + 1) * 128,
                                                tm:tm + 512],
                                        in_=r_t3[:, ec, :]),
                                    key=("dbg", t, ec), q=16)
                        sm["s_rv"].wait(se, ("rv", t))
                        v = stag(se.activation(out=vrow[:], in_=vrow[:], func=AF.Ln,
                                               bias=1e-5))
                        swait(v)
                        sm["s_rs"].inc(
                            se.activation(out=vrow[:], in_=vrow[:], func=AF.Exp,
                                          scale=-0.5),
                            key=("rs", t))
                        if stage >= 2:
                            tm = t * 512
                            for ec in range(EC):
                                ge = t * EC + ec
                                sm["s_x1"].wait(se, ("x1", t, ec))
                                if ge >= 2:
                                    pt, pe = divmod(ge - 2, EC)
                                    sm[f"s_x1w{(ge - 2) % 4}"].wait(se,
                                                                   ("x1w", pt, pe))
                                cast_i = se.activation(out=x1c_b[ec % 2][:],
                                                       in_=x1_b[ec % 2][:],
                                                       func=AF.Copy)
                                sm["s_x1b"].inc(cast_i, key=("x1b", t, ec))
                                se.wait_ge(sm["s_x1b"].h, sm["s_x1b"].n)
                                sm[f"s_x1w{ge % 4}"].inc(
                                    se.dma_start(
                                        out=x1f[ec * 128:(ec + 1) * 128,
                                                tm:tm + 512],
                                        in_=x1_b[ec % 2][:]), q=16)
                                sm[f"s_x1w{ge % 4}"].inc(
                                    se.dma_start(
                                        out=x1bd[ec * 128:(ec + 1) * 128,
                                                 tm:tm + 512],
                                        in_=x1c_b[ec % 2][:]),
                                    key=("x1w", t, ec), q=16)
                                if stage == 2:
                                    sm[f"s_dbg{ec % 2}"].inc(
                                        se.dma_start(
                                            out=out[ec * 128:(ec + 1) * 128,
                                                    tm:tm + 512],
                                            in_=x1_b[ec % 2][:]),
                                        key=("dbg", t, ec), q=16)
                        if stage < 3:
                            sm["s_dbg0"].mark(("dbgt", t))
                            sm["s_dbg1"].mark(("dbgt", t))
                    if stage >= 3:
                        for sl in range(4):
                            sm[f"s_x1w{sl}"].mark("fin")
                    if stage < 3:
                        sm["s_dbg0"].wait(se, ("dbgt", NT - 1))
                        sm["s_dbg1"].wait(se, ("dbgt", NT - 1))

                @blk.vector
                def _(ve):
                    vs = sm["s_vself"]

                    def vtag(instr):
                        vs.inc(instr)
                        return vs.n

                    def vwait(v):
                        ve.wait_ge(vs.h, v)

                    sm["s_const"].wait(ve, "consts")
                    sm["s_sinit"].wait(ve, "sinit")
                    v = vtag(ve.tensor_add(A_t[:], A_t[:], spl_t[:]))
                    vwait(v)
                    vtag(ve.tensor_scalar_mul(A_t[:], A_t[:], -1.0))
                    vtag(ve.tensor_mul(cbb_t[:], cbb_t[:], cb2_t[:]))
                    vtag(ve.tensor_scalar_add(dp1_t[:], dp1_t[:], 1.0))
                    vtag(ve.memset(onesc_t[:], 1.0))
                    vwait(vs.n)
                    sm["s_vinit"].inc(ve.memset(onesr_t[:], 1.0), key="vinit")
                    sm["s_vinit"].wait(ve, "vinit")
                    vsub_box = [0]

                    def prep_sub():
                        pi = vsub_box[0]
                        if pi >= NPREP:
                            return
                        mat, _b = PREP_CHUNKS[pi]
                        w_ = II if mat in ("g", "u") else D
                        sm[f"s_pi{pi % 2}"].wait(ve, ("pi", pi))
                        sm["s_ps"].inc(
                            ve.tensor_scalar(out=pstg[pi % 2][:, 0:w_],
                                             in0=pstg[pi % 2][:, 0:w_],
                                             scalar1=-128.0, scalar2=None,
                                             op0=ALU.add),
                            key=("sub", pi))
                        vsub_box[0] += 1
                    for t, (u0, St, wm) in enumerate(TILES):
                        rngs = _rngs(wm)
                        if t > 0:
                            vwait(vs.n)
                            if stage < 3:
                                sm["s_dbg0"].wait(ve, ("dbgt", t - 1))
                                sm["s_dbg1"].wait(ve, ("dbgt", t - 1))
                        for ec in range(EC):
                            ge = t * EC + ec
                            for _ in range(3):
                                prep_sub()
                            for ri, (off, W_) in enumerate(rngs):
                                sm["s_sp"].wait(ve, ("sp", t, ec, ri))
                                if ri == 0 and ge >= 2:
                                    pt, pe = divmod(ge - 2, EC)
                                    sm["s_ab"].wait(ve, ("ab", pt, pe, N - 1))
                                v = vtag(ve.tensor_scalar(
                                    out=dt_b[ec % 2][:, off:off + W_],
                                    in0=spA[ec % 2][:, off:off + W_],
                                    scalar1=eps4_t[:, 1:2], scalar2=None,
                                    op0=ALU.add))
                                sm["s_prj"].wait(ve, ("prj", t, ec, ri, 1))
                                vwait(v)
                                v = vtag(ve.scalar_tensor_tensor(
                                    out=w_b[ec % 2][:, off:off + W_],
                                    in0=psP[1][:, 0:W_], scalar=bb_t[:, ec:ec + 1],
                                    in1=dt_b[ec % 2][:, off:off + W_],
                                    op0=ALU.add, op1=ALU.mult))
                                sm[f"s_uch{ec % 2}"].wait(ve, ("uch", t, ec))
                                vwait(v)
                                sm["s_w"].inc(
                                    ve.tensor_mul(w_b[ec % 2][:, off:off + W_],
                                                  w_b[ec % 2][:, off:off + W_],
                                                  u_ch[ec % 2][:, off:off + W_]),
                                    key=("w", t, ec, ri))
                            M0 = wm
                            last = None
                            for n in range(N):
                                sm["s_ab"].wait(ve, ("ab", t, ec, n))
                                init = (0.0 if t == 0
                                        else carry[:, ec * N + n:ec * N + n + 1])
                                sci = ve.tensor_tensor_scan(
                                    out=xp_b[n % 2][:, 0:St],
                                    data0=ab_b[n % 4][:, 0:St],
                                    data1=w_b[ec % 2][:, 0:St],
                                    initial=init, op0=ALU.mult, op1=ALU.add)
                                sm["s_scan"].inc(sci, key=("scan", t, ec, n))
                                ve.wait_ge(sm["s_scan"].h, sm["s_scan"].n)
                                if n > 0:
                                    vwait(vs.n)
                                vtag(ve.tensor_copy(
                                    carry[:, ec * N + n:ec * N + n + 1],
                                    xp_b[n % 2][:, St - 1:St]))
                                if n == 0:
                                    last = ve.tensor_scalar(
                                        out=ya_b[ec % 2][:],
                                        in0=xp_b[0][:, M0:M0 + 512],
                                        scalar1=cbb_t[:, ec * N:ec * N + 1],
                                        scalar2=None, op0=ALU.mult)
                                else:
                                    last = ve.scalar_tensor_tensor(
                                        out=ya_b[ec % 2][:],
                                        in0=xp_b[n % 2][:, M0:M0 + 512],
                                        scalar=cbb_t[:, ec * N + n:ec * N + n + 1],
                                        in1=ya_b[ec % 2][:], op0=ALU.mult,
                                        op1=ALU.add)
                                vtag(last)

                            sm["s_cE"].wait(ve, ("cE", t, ec))
                            vwait(vs.n)
                            v = vtag(ve.tensor_mul(ya_b[ec % 2][:], ya_b[ec % 2][:],
                                                   c_b[ec % 2][:]))
                            if t > 0:
                                sm["s_st"].wait(ve, ("st", t - 1, ec))
                                sm["s_sq"].wait(ve, ("sq", t - 1, ec))
                            vwait(v)
                            sm["s_r"].inc(
                                ve.scalar_tensor_tensor(
                                    out=r_t3[:, ec, :],
                                    in0=u_ch[ec % 2][:, M0:M0 + 512],
                                    scalar=dp1_t[:, ec:ec + 1],
                                    in1=ya_b[ec % 2][:], op0=ALU.mult, op1=ALU.add),
                                key=("r", t, ec))
                        sm["s_st"].wait(ve, ("st", t, EC - 1))
                        v = vtag(ve.tensor_scalar(out=mrow[:], in0=psSm[:],
                                                  scalar1=1.0 / D,
                                                  scalar2=None, op0=ALU.mult))
                        vwait(v)
                        v = vtag(ve.tensor_mul(vrow[:], mrow[:], mrow[:]))
                        vwait(v)
                        sm["s_rv"].inc(
                            ve.scalar_tensor_tensor(out=vrow[:], in0=psSq[:],
                                                    scalar=1.0 / D, in1=vrow[:],
                                                    op0=ALU.mult, op1=ALU.subtract),
                            key=("rv", t))
                        if stage >= 2:
                            sm["s_bc"].wait(ve, ("bc", t))
                            for ec in range(EC):
                                ge = t * EC + ec
                                if ge >= 2:
                                    pt, pe = divmod(ge - 2, EC)
                                    sm["s_x1b"].wait(ve, ("x1b", pt, pe))
                                    sm[f"s_x1w{(ge - 2) % 4}"].wait(ve, ("x1w", pt, pe))
                                    if stage == 2:
                                        sm[f"s_dbg{pe % 2}"].wait(ve, ("dbg", pt, pe))
                                v = vtag(ve.tensor_sub(x1_b[ec % 2][:],
                                                       r_t3[:, ec, :], psX[:]))
                                vwait(v)
                                sm["s_x1"].inc(
                                    ve.tensor_mul(x1_b[ec % 2][:], x1_b[ec % 2][:],
                                                  psIV[:]),
                                    key=("x1", t, ec))
                        else:
                            for ec in range(EC):
                                sm["s_x1"].inc(ve.nop(), key=("x1", t, ec))
                    while vsub_box[0] < NPREP:
                        prep_sub()

        if stage >= 3:
            with ExitStack() as cf:
                fsb = lambda nm, sh, dt: cf.enter_context(nc.sbuf_tensor(nm, sh, dt))
                fps = lambda nm, sh: cf.enter_context(nc.psum_tensor(nm, sh, F32))
                x1n = fsb("x1n", [128, EC * 512], BF16)
                x1n3 = x1n[:].rearrange("p (k t) -> p k t", k=EC)
                gstat = [fsb(f"gstat{i}", [128, EC * 2 * 128], BF16) for i in range(2)]
                gstat4 = [g[:].rearrange("p (k i e) -> p k i e", k=EC, i=2)
                          for g in gstat]
                ustat = [fsb(f"ustat{i}", [128, EC * 2 * 128], BF16) for i in range(2)]
                ustat4 = [u_[:].rearrange("p (k i e) -> p k i e", k=EC, i=2)
                          for u_ in ustat]
                h_t = fsb("h_t", [128, IC * 512], BF16)
                h_t3 = h_t[:].rearrange("p (k t) -> p k t", k=IC)
                dstat = [fsb(f"dstat{i}", [128, IC * 128], BF16) for i in range(2)]
                dstat3 = [d[:].rearrange("p (k e) -> p k e", k=IC) for d in dstat]
                sg_b = [fsb(f"sg_b{i}", [128, 512], F32) for i in range(2)]
                x1f_ch = [fsb(f"x1f_ch{i}", [128, 512], F32) for i in range(2)]
                r2_b = [fsb(f"r2_b{i}", [128, 512], F32) for i in range(2)]
                sq2_b = [fsb(f"sq2_b{i}", [128, 512], F32) for i in range(2)]
                r2r_b = [fsb(f"r2r_b{i}", [128, 512], F32) for i in range(2)]
                x2_b = [fsb(f"x2_b{i}", [128, 512], F32) for i in range(2)]
                psG = [fps(f"psG{j}", [128, 512]) for j in range(2)]
                psU = fps("psU", [128, 512])
                psD = fps("psD", [128, 512])
                psS2m = fps("psS2m", [1, 512])
                psS2q = fps("psS2q", [1, 512])
                psM2 = fps("psM2", [128, 512])
                psV2 = fps("psV2", [128, 512])

                with nc.Block() as fbk:
                    @fbk.gpsimd
                    def _(gp):
                        for ts in range(4):
                            tm = ts * 512
                            if ts == 0:
                                for sl in range(4):
                                    sm[f"s_x1w{sl}"].wait(gp, "fin")
                                for nmm in ("s_pog0", "s_pog1", "s_pou0",
                                            "s_pou1", "s_pod0", "s_pod1"):
                                    sm[nmm].wait(gp, "fin")
                            else:
                                sm["s_fu"].wait(gp, ("fu", ts - 1, IC - 1))
                            sm["s_fx"].inc(
                                gp.dma_start(
                                    out=x1n3[:, :, :],
                                    in_=x1bd[:, tm:tm + 512]
                                    .rearrange("(k p) t -> p k t", p=128)),
                                key=("fx", ts), q=16)
                            for icg in range(IC // 2):
                                rot = icg % 2
                                gi = ts * (IC // 2) + icg
                                if gi >= 2:
                                    pts, pik = divmod(gi - 2, IC // 2)
                                    sm["s_fu"].wait(gp, ("fu", pts, pik * 2 + 1))
                                i0 = icg * 256
                                sm[f"s_fgs{rot}"].inc(
                                    gp.dma_start(
                                        out=gstat4[rot][:, :, :, :],
                                        in_=gwbT[:, i0:i0 + 256].rearrange(
                                            "(k p) (i e) -> p k i e", p=128, i=2)),
                                    key=("fgsg", ts, icg), q=16)
                                sm[f"s_fus{rot}"].inc(
                                    gp.dma_start(
                                        out=ustat4[rot][:, :, :, :],
                                        in_=uwbT[:, i0:i0 + 256].rearrange(
                                            "(k p) (i e) -> p k i e", p=128, i=2)),
                                    key=("fgs", ts, icg), q=16)

                    @fbk.sync
                    def _(sy):
                        for sl in range(4):
                            sm[f"s_x1w{sl}"].wait(sy, "fin")
                        sm["s_pod0"].wait(sy, "fin")
                        sm["s_pod1"].wait(sy, "fin")
                        for ts in range(4):
                            tm = ts * 512
                            for hc in range(EC):
                                gh = ts * EC + hc
                                if gh >= 2:
                                    pts, phc = divmod(gh - 2, EC)
                                    sm["s_fdm"].wait(sy, ("fdm", pts, phc))
                                sm[f"s_fds{hc % 2}"].inc(
                                    sy.dma_start(
                                        out=dstat3[hc % 2][:, :, :],
                                        in_=dwbT[:, hc * 128:(hc + 1) * 128]
                                        .rearrange("(k p) e -> p k e", p=128)),
                                    key=("fds", ts, hc), q=16)
                                if gh >= 2:
                                    pts, phc = divmod(gh - 2, EC)
                                    sm["s_fr2"].wait(sy, ("fr2", pts, phc))
                                sm[f"s_fx1r{hc % 2}"].inc(
                                    sy.dma_start(
                                        out=x1f_ch[hc % 2][:],
                                        in_=x1f[hc * 128:(hc + 1) * 128,
                                                tm:tm + 512]),
                                    key=("fx1r", ts, hc), q=16)
                                if hc >= 1:
                                    sm["s_fr2"].wait(sy, ("fr2", ts, hc - 1))
                                    sm[f"s_fr2w{(hc - 1) % 2}"].inc(
                                        sy.dma_start(
                                            out=r2f[(hc - 1) * 128:hc * 128,
                                                    tm:tm + 512],
                                            in_=r2_b[(hc - 1) % 2][:]),
                                        key=("fr2w", ts, hc - 1), q=16)
                            sm["s_fr2"].wait(sy, ("fr2", ts, EC - 1))
                            sm[f"s_fr2w{(EC - 1) % 2}"].inc(
                                sy.dma_start(
                                    out=r2f[(EC - 1) * 128:EC * 128, tm:tm + 512],
                                    in_=r2_b[(EC - 1) % 2][:]),
                                key=("fr2w", ts, EC - 1), q=16)
                            sm["s_fr2w0"].mark(("r2wt", ts))
                            sm["s_fr2w1"].mark(("r2wt", ts))
                            for hc in range(EC):
                                gh = ts * EC + hc
                                if gh >= 2:
                                    pts, phc = divmod(gh - 2, EC)
                                    sm["s_fx2"].wait(sy, ("fx2", pts, phc))
                                sm["s_fr2w0"].wait(sy, ("r2wt", ts))
                                sm["s_fr2w1"].wait(sy, ("r2wt", ts))
                                sm[f"s_fr2r{hc % 2}"].inc(
                                    sy.dma_start(
                                        out=r2r_b[hc % 2][:],
                                        in_=r2f[hc * 128:(hc + 1) * 128,
                                                tm:tm + 512]),
                                    key=("fr2r", ts, hc), q=16)
                                sm["s_fx2"].wait(sy, ("fx2", ts, hc))
                                sm[f"s_fo{hc % 2}"].inc(
                                    sy.dma_start(
                                        out=out[hc * 128:(hc + 1) * 128,
                                                tm:tm + 512],
                                        in_=x2_b[hc % 2][:]),
                                    key=("fo", ts, hc), q=16)
                        sy.wait_ge(sm["s_fo0"].h, sm["s_fo0"].n)
                        sy.wait_ge(sm["s_fo1"].h, sm["s_fo1"].n)

                    @fbk.tensor
                    def _(te):
                        def stats2(ts_, hc_):
                            if hc_ == 0 and ts_ > 0:
                                sm["s_frv"].wait(te, ("frv", ts_ - 1))
                            sm["s_fr2"].wait(te, ("fr2", ts_, hc_))
                            sm["s_fsq"].wait(te, ("fsq", ts_, hc_))
                            te.matmul(psS2m[:], onesc_t[:], r2_b[hc_ % 2][:],
                                      start=(hc_ == 0), stop=(hc_ == EC - 1))
                            sm["s_fst"].inc(
                                te.matmul(psS2q[:], onesc_t[:], sq2_b[hc_ % 2][:],
                                          start=(hc_ == 0), stop=(hc_ == EC - 1)),
                                key=("fst", ts_, hc_))

                        for ts in range(4):
                            sm["s_fx"].wait(te, ("fx", ts))
                            for icg in range(IC // 2):
                                rot = icg % 2
                                sm[f"s_fgs{rot}"].wait(te, ("fgsg", ts, icg))
                                sm[f"s_fus{rot}"].wait(te, ("fgs", ts, icg))
                                for ii in range(2):
                                    ic = icg * 2 + ii
                                    gi2 = ts * IC + ic
                                    if gi2 >= 2:
                                        pts, pic = divmod(gi2 - 2, IC)
                                        sm["s_fsg"].wait(te, ("fsg", pts, pic))
                                    mm = None
                                    for k in range(EC):
                                        mm = te.matmul(
                                            psG[ic % 2][:],
                                            gstat4[rot][:, k, ii, :],
                                            x1n3[:, k, :],
                                            start=(k == 0), stop=(k == EC - 1))
                                    sm["s_fg"].inc(mm, key=("fg", ts, ic))
                                    if gi2 >= 1:
                                        pts, pic = divmod(gi2 - 1, IC)
                                        sm["s_fh"].wait(te, ("fh", pts, pic))
                                    mm = None
                                    for k in range(EC):
                                        mm = te.matmul(
                                            psU[:],
                                            ustat4[rot][:, k, ii, :],
                                            x1n3[:, k, :],
                                            start=(k == 0), stop=(k == EC - 1))
                                    sm["s_fu"].inc(mm, key=("fu", ts, ic))
                            for hc in range(EC):
                                if hc >= 2:
                                    stats2(ts, hc - 2)
                                sm[f"s_fds{hc % 2}"].wait(te, ("fds", ts, hc))
                                sm["s_fh"].wait(te, ("fh", ts, IC - 1))
                                if hc >= 1:
                                    sm["s_fr2"].wait(te, ("fr2", ts, hc - 1))
                                elif ts > 0:
                                    sm["s_fr2"].wait(te, ("fr2", ts - 1, EC - 1))
                                mm = None
                                for k in range(IC):
                                    mm = te.matmul(
                                        psD[:], dstat3[hc % 2][:, k, :],
                                        h_t3[:, k, :],
                                        start=(k == 0), stop=(k == IC - 1))
                                sm["s_fdm"].inc(mm, key=("fdm", ts, hc))
                            stats2(ts, EC - 2)
                            stats2(ts, EC - 1)
                            sm["s_frv"].wait(te, ("frv", ts))
                            sm["s_frs"].wait(te, ("frs", ts))
                            if ts > 0:
                                sm["s_fx2"].wait(te, ("fx2", ts - 1, EC - 1))
                            te.matmul(psM2[:], onesr_t[:], mrow[:],
                                      start=True, stop=True)
                            sm["s_fbc"].inc(
                                te.matmul(psV2[:], onesr_t[:], vrow[:],
                                          start=True, stop=True),
                                key=("fbc", ts))

                    @fbk.scalar
                    def _(se):
                        ss = sm["s_sself"]
                        for nmm in ("s_pog0", "s_pog1", "s_pou0", "s_pou1",
                                    "s_pod0", "s_pod1"):
                            sm[nmm].wait(se, "fin")
                        for ts in range(4):
                            for ic in range(IC):
                                gi2 = ts * IC + ic
                                sm["s_fg"].wait(se, ("fg", ts, ic))
                                if gi2 >= 2:
                                    pts, pic = divmod(gi2 - 2, IC)
                                    sm["s_fh"].wait(se, ("fh", pts, pic))
                                sm["s_fsg"].inc(
                                    se.activation(out=sg_b[ic % 2][:],
                                                  in_=psG[ic % 2][:],
                                                  func=AF.Silu,
                                                  scale=gs_t[:, ic:ic + 1]),
                                    key=("fsg", ts, ic))
                            for hc in range(EC):
                                gh = ts * EC + hc
                                sm["s_fr2"].wait(se, ("fr2", ts, hc))
                                if gh >= 2:
                                    pts, phc = divmod(gh - 2, EC)
                                    sm["s_fst"].wait(se, ("fst", pts, phc))
                                sm["s_fsq"].inc(
                                    se.activation(out=sq2_b[hc % 2][:],
                                                  in_=r2_b[hc % 2][:],
                                                  func=AF.Square),
                                    key=("fsq", ts, hc))
                            sm["s_frv"].wait(se, ("frv", ts))
                            ss.inc(se.activation(out=vrow[:], in_=vrow[:],
                                                 func=AF.Ln, bias=1e-5))
                            se.wait_ge(ss.h, ss.n)
                            sm["s_frs"].inc(
                                se.activation(out=vrow[:], in_=vrow[:],
                                              func=AF.Exp, scale=-0.5),
                                key=("frs", ts))

                    @fbk.vector
                    def _(ve):
                        vs = sm["s_vself"]
                        for nmm in ("s_pog0", "s_pog1", "s_pou0", "s_pou1",
                                    "s_pod0", "s_pod1"):
                            sm[nmm].wait(ve, "fin")
                        for ts in range(4):
                            for ic in range(IC):
                                gi2 = ts * IC + ic
                                sm["s_fu"].wait(ve, ("fu", ts, ic))
                                sm["s_fsg"].wait(ve, ("fsg", ts, ic))
                                if ts > 0 and ic == 0:
                                    sm["s_fdm"].wait(ve, ("fdm", ts - 1, EC - 1))
                                sm["s_fh"].inc(
                                    ve.scalar_tensor_tensor(
                                        out=h_t3[:, ic, :], in0=psU[:],
                                        scalar=us_t[:, ic:ic + 1],
                                        in1=sg_b[ic % 2][:],
                                        op0=ALU.mult, op1=ALU.mult),
                                    key=("fh", ts, ic))
                            for hc in range(EC):
                                gh = ts * EC + hc
                                sm["s_fdm"].wait(ve, ("fdm", ts, hc))
                                sm[f"s_fx1r{hc % 2}"].wait(ve, ("fx1r", ts, hc))
                                if gh >= 2:
                                    pts, phc = divmod(gh - 2, EC)
                                    sm[f"s_fr2w{phc % 2}"].wait(ve,
                                                                ("fr2w", pts, phc))
                                    sm["s_fst"].wait(ve, ("fst", pts, phc))
                                sm["s_fr2"].inc(
                                    ve.scalar_tensor_tensor(
                                        out=r2_b[hc % 2][:], in0=psD[:],
                                        scalar=ds_t[:, hc:hc + 1],
                                        in1=x1f_ch[hc % 2][:],
                                        op0=ALU.mult, op1=ALU.add),
                                    key=("fr2", ts, hc))
                            sm["s_fst"].wait(ve, ("fst", ts, EC - 1))
                            v = vs.n + 1
                            vs.inc(ve.tensor_scalar(out=mrow[:], in0=psS2m[:],
                                                    scalar1=1.0 / D, scalar2=None,
                                                    op0=ALU.mult))
                            ve.wait_ge(vs.h, v)
                            v = vs.n + 1
                            vs.inc(ve.tensor_mul(vrow[:], mrow[:], mrow[:]))
                            ve.wait_ge(vs.h, v)
                            sm["s_frv"].inc(
                                ve.scalar_tensor_tensor(out=vrow[:], in0=psS2q[:],
                                                        scalar=1.0 / D, in1=vrow[:],
                                                        op0=ALU.mult,
                                                        op1=ALU.subtract),
                                key=("frv", ts))
                            sm["s_fbc"].wait(ve, ("fbc", ts))
                            for hc in range(EC):
                                gh = ts * EC + hc
                                sm[f"s_fr2r{hc % 2}"].wait(ve, ("fr2r", ts, hc))
                                if gh >= 2:
                                    pts, phc = divmod(gh - 2, EC)
                                    sm[f"s_fo{phc % 2}"].wait(ve, ("fo", pts, phc))
                                v = vs.n + 1
                                vs.inc(ve.tensor_sub(x2_b[hc % 2][:],
                                                     r2r_b[hc % 2][:], psM2[:]))
                                ve.wait_ge(vs.h, v)
                                sm["s_fx2"].inc(
                                    ve.tensor_mul(x2_b[hc % 2][:], x2_b[hc % 2][:],
                                                  psV2[:]),
                                    key=("fx2", ts, hc))

        tables = {name: dict(s.keys) for name, s in sm.items()}
    return nc, tables


def build(stage=3):
    _, tables = build_nc(stage=stage, table=None)
    nc, t2 = build_nc(stage=stage, table=tables)
    assert t2 == tables, "nondeterministic build"
    return nc, tables


def make_inmaps(inputs):
    u = np.asarray(inputs["L0_attn_u"], np.float32)

    def col(v):
        return np.ascontiguousarray(np.asarray(v, np.float32).reshape(EC, 128).T)

    def colN(v):
        v = np.asarray(v, np.float32).reshape(EC, 128, N)
        return np.ascontiguousarray(v.transpose(1, 0, 2).reshape(128, EC * N))

    wmix_arr = np.stack([np.asarray(inputs["L0_attn_Wdt"], np.float32),
                         np.asarray(inputs["L0_attn_Wb"], np.float32),
                         np.asarray(inputs["L0_attn_Wc"], np.float32)])
    wmix_arr = np.ascontiguousarray(
        wmix_arr.reshape(3, EC, 128, EC, 128).transpose(3, 2, 0, 1, 4)
        .reshape(D, 3 * EC * 128))
    com = {
        "wmix": wmix_arr,
        "bdt": col(inputs["L0_attn_bdt"]), "bb": col(inputs["L0_attn_bb"]),
        "bc": col(inputs["L0_attn_bc"]), "araw": colN(inputs["L0_attn_a_raw"]),
        "bbase": colN(inputs["L0_attn_B_base"]),
        "cbase": colN(inputs["L0_attn_C_base"]),
        "dvec": col(inputs["L0_attn_D"]),
        "cin": np.zeros((1, 8), np.float32),
        "gw": np.ascontiguousarray(np.asarray(inputs["L0_ffn_gate_w"], np.int32).T),
        "uw": np.ascontiguousarray(np.asarray(inputs["L0_ffn_up_w"], np.int32).T),
        "dw": np.ascontiguousarray(np.asarray(inputs["L0_ffn_down_w"], np.int32).T),
        "gs": np.ascontiguousarray(
            np.asarray(inputs["L0_ffn_gate_scale"], np.float32).reshape(IC, 128).T),
        "us": np.ascontiguousarray(
            np.asarray(inputs["L0_ffn_up_scale"], np.float32).reshape(IC, 128).T),
        "ds": np.ascontiguousarray(
            np.asarray(inputs["L0_ffn_down_scale"], np.float32).reshape(EC, 128).T),
        "ccol": np.ascontiguousarray(
            np.tile(np.array([[-128.0, 1e-4]], np.float32), (128, 1))),
    }
    maps = []
    for c in range(8):
        b, half = c // 2, c % 2
        useg = np.zeros((WARM + TOK, D), np.float32)
        start = half * TOK
        useg[WARM:] = u[b, start:start + TOK]
        if half == 1:
            useg[:WARM] = u[b, start - WARM:start]
        maps.append({"u": np.ascontiguousarray(useg.T), **com})
    return maps


def kernel(**inputs):
    from concourse.bass_utils import run_bass_kernel_spmd

    maps = make_inmaps(inputs)
    nc, _tables = build(stage=3)
    res = run_bass_kernel_spmd(nc, maps, core_ids=list(range(8)))
    out = np.empty((4, 4096, 2048), np.float32)
    for c in range(8):
        b, half = c // 2, c % 2
        got = np.asarray(res.results[c]["out"])
        out[b, half * TOK:(half + 1) * TOK, :] = got.T
    return out


# version 08
# speedup vs baseline: 1.0501x; 1.0181x over previous
import numpy as np
from contextlib import ExitStack

import concourse.bass as bass
import concourse.mybir as mybir

F32 = mybir.dt.float32
BF16 = mybir.dt.bfloat16
I32 = mybir.dt.int32
AF = mybir.ActivationFunctionType
ALU = mybir.AluOpType

D, N, II = 2048, 16, 8192
WARM, TOK = 256, 2048
EC = D // 128
IC = II // 128
TILES = [(0, 768, 256), (768, 512, 0), (1280, 512, 0), (1792, 512, 0)]
NT = len(TILES)


def _rngs(wm):
    return [(0, wm), (wm, 512)] if wm else [(0, 512)]


class Sem:
    def __init__(self, nc, ctx, name, table):
        self.h = ctx.enter_context(nc.semaphore(name))
        self.name, self.n, self.keys, self.table = name, 0, {}, table

    def inc(self, instr, key=None, q=1):
        self.n += q
        if key is not None:
            assert key not in self.keys, (self.name, key)
            self.keys[key] = self.n
        instr.then_inc(self.h, q)

    def mark(self, key):
        self.keys[key] = self.n

    def at(self, key):
        src = self.table[self.name] if self.table is not None else self.keys
        return src.get(key)

    def wait(self, eng, key, required=True):
        v = self.at(key)
        if v is None:
            if required and self.table is not None:
                raise KeyError(f"sem {self.name}: missing key {key}")
            return
        eng.wait_ge(self.h, v)


SEM_NAMES = [
    "s_const", "s_uch0", "s_uch1", "s_ubf", "s_ws0", "s_ws1",
    "s_pi0", "s_pi1", "s_ps", "s_pog0", "s_pog1", "s_pou0", "s_pou1", "s_pod0", "s_pod1",
    "s_sinit", "s_vinit",
    "s_prj", "s_sp", "s_w", "s_cE", "s_ab", "s_scan", "s_y", "s_r",
    "s_sq", "s_st", "s_rv", "s_rs", "s_bc", "s_x1", "s_x1b", "s_x1w0", "s_x1w1", "s_x1w2", "s_x1w3", "s_dbg0", "s_dbg1",
    "s_fx", "s_fgs0", "s_fgs1", "s_fus0", "s_fus1", "s_fds0", "s_fds1", "s_fg", "s_fu", "s_fsg", "s_fh",
    "s_fdm", "s_fr2", "s_fr2w0", "s_fr2w1", "s_fx1r0", "s_fx1r1",
    "s_fr2r0", "s_fr2r1", "s_fsq", "s_fst",
    "s_frv", "s_frs", "s_fbc", "s_fx2", "s_fo0", "s_fo1",
    "s_vself", "s_sself",
]

PREP_CHUNKS = ([("g", i) for i in range(EC)] + [("u", i) for i in range(EC)]
               + [("d", i) for i in range(IC)])
NPREP = len(PREP_CHUNKS)


def _po_sem(pi):
    return f"s_po{PREP_CHUNKS[pi][0]}{pi % 2}"
NCONSTD = 12


def build_nc(stage=3, table=None):
    nc = bass.Bass("TRN2", target_bir_lowering=False)

    u_in = nc.dram_tensor("u", [D, WARM + TOK], F32, kind="ExternalInput").ap()
    wmix = nc.dram_tensor("wmix", [D, 3 * EC * 128], F32, kind="ExternalInput").ap()
    bdt = nc.dram_tensor("bdt", [128, EC], F32, kind="ExternalInput").ap()
    bbi = nc.dram_tensor("bb", [128, EC], F32, kind="ExternalInput").ap()
    bci = nc.dram_tensor("bc", [128, EC], F32, kind="ExternalInput").ap()
    araw = nc.dram_tensor("araw", [128, EC * N], F32, kind="ExternalInput").ap()
    bbase = nc.dram_tensor("bbase", [128, EC * N], F32, kind="ExternalInput").ap()
    cbase = nc.dram_tensor("cbase", [128, EC * N], F32, kind="ExternalInput").ap()
    dvec = nc.dram_tensor("dvec", [128, EC], F32, kind="ExternalInput").ap()
    cin = nc.dram_tensor("cin", [1, 8], F32, kind="ExternalInput").ap()
    ccol = nc.dram_tensor("ccol", [128, 2], F32, kind="ExternalInput").ap()
    gw = nc.dram_tensor("gw", [D, II], I32, kind="ExternalInput").ap()
    uwq = nc.dram_tensor("uw", [D, II], I32, kind="ExternalInput").ap()
    dwq = nc.dram_tensor("dw", [II, D], I32, kind="ExternalInput").ap()
    gsi = nc.dram_tensor("gs", [128, IC], F32, kind="ExternalInput").ap()
    usi = nc.dram_tensor("us", [128, IC], F32, kind="ExternalInput").ap()
    dsi = nc.dram_tensor("ds", [128, EC], F32, kind="ExternalInput").ap()

    out = nc.dram_tensor("out", [D, TOK], F32, kind="ExternalOutput").ap()

    gwbT = nc.dram_tensor("gwbT", [D, II], BF16, kind="Internal").ap()
    uwbT = nc.dram_tensor("uwbT", [D, II], BF16, kind="Internal").ap()
    dwbT = nc.dram_tensor("dwbT", [II, D], BF16, kind="Internal").ap()
    x1f = nc.dram_tensor("x1f", [D, TOK], F32, kind="Internal").ap()
    x1bd = nc.dram_tensor("x1bd", [D, TOK], BF16, kind="Internal").ap()
    r2f = nc.dram_tensor("r2f", [D, TOK], F32, kind="Internal").ap()

    with ExitStack() as ctx0:
        csb = lambda nm, sh, dt: ctx0.enter_context(nc.sbuf_tensor(nm, sh, dt))
        sm = {name: Sem(nc, ctx0, name, table) for name in SEM_NAMES}

        cst = csb("cst", [1, 8], F32)
        nc.const_aps.register(F32, 1e-5, cst[0:1, 0:1])
        nc.const_aps.register(F32, -0.5, cst[0:1, 1:2])
        ccol_t = csb("ccol_t", [128, 2], F32)
        nc.const_aps.register(F32, -128.0, ccol_t[:, 0:1])
        eps4_t = ccol_t
        bdt_t = csb("bdt_t", [128, EC], F32)
        bb_t = csb("bb_t", [128, EC], F32)
        bc_t = csb("bc_t", [128, EC], F32)
        araw_t = csb("araw_t", [128, EC * N], F32)
        A_t = csb("A_t", [128, EC * N], F32)
        spl_t = csb("spl_t", [128, EC * N], F32)
        cbb_t = csb("cbb_t", [128, EC * N], F32)
        cb2_t = csb("cb2_t", [128, EC * N], F32)
        dp1_t = csb("dp1_t", [128, EC], F32)
        gs_t = csb("gs_t", [128, IC], F32)
        us_t = csb("us_t", [128, IC], F32)
        ds_t = csb("ds_t", [128, EC], F32)
        onesc_t = csb("onesc_t", [128, 1], F32)
        onesr_t = csb("onesr_t", [1, 128], F32)
        carry = csb("carry", [128, EC * N], F32)
        mrow = csb("mrow", [1, 512], F32)
        vrow = csb("vrow", [1, 512], F32)

        with ExitStack() as cm:
            msb = lambda nm, sh, dt: cm.enter_context(nc.sbuf_tensor(nm, sh, dt))
            mps = lambda nm, sh: cm.enter_context(nc.psum_tensor(nm, sh, F32))
            S0 = TILES[0][1]
            u_bf = msb("u_bf", [128, EC * S0], BF16)
            u_bf3 = u_bf[:].rearrange("p (k s) -> p k s", k=EC)
            u_ch = [msb(f"u_ch{i}", [128, S0], F32) for i in range(2)]
            dt_b = [msb(f"dt_b{i}", [128, S0], F32) for i in range(2)]
            w_b = [msb(f"w_b{i}", [128, S0], F32) for i in range(2)]
            c_b = [msb(f"c_b{i}", [128, 512], F32) for i in range(2)]
            spA = [msb(f"spA{i}", [128, S0], F32) for i in range(2)]
            ab_b = [msb(f"ab_b{i}", [128, S0], F32) for i in range(4)]
            xp_b = [msb(f"xp_b{i}", [128, S0], F32) for i in range(2)]
            ya_b = [msb(f"ya_b{i}", [128, 512], F32) for i in range(2)]
            r_t = msb("r_t", [128, EC * 512], F32)
            r_t3 = r_t[:].rearrange("p (k s) -> p k s", k=EC)
            x1_b = [msb(f"x1_b{i}", [128, 512], F32) for i in range(2)]
            x1c_b = [msb(f"x1c_b{i}", [128, 512], BF16) for i in range(2)]
            wstat = [msb(f"wstat{i}", [128, 3 * EC * 128], BF16) for i in range(2)]
            wstat4 = [w[:].rearrange("p (j k e) -> p j k e", j=3, k=EC) for w in wstat]
            pstg = [msb(f"pstg{i}", [128, II], BF16) for i in range(2)]

            psP = [mps(f"psP{j}", [128, 512]) for j in range(3)]
            psSm = mps("psSm", [1, 512])
            psSq = mps("psSq", [1, 512])
            psX = mps("psX", [128, 512])
            psIV = mps("psIV", [128, 512])

            with nc.Block() as blk:
                @blk.sync
                def _(sy):
                    for dst, src in [
                        (cst[:], cin), (bdt_t[:], bdt), (bb_t[:], bbi),
                        (bc_t[:], bci), (araw_t[:], araw), (cbb_t[:], bbase),
                        (cb2_t[:], cbase), (dp1_t[:], dvec), (gs_t[:], gsi),
                        (us_t[:], usi), (ds_t[:], dsi), (ccol_t[:], ccol),
                    ]:
                        sm["s_const"].inc(sy.dma_start(out=dst, in_=src), q=16)
                    sm["s_const"].mark("consts")
                    assert sm["s_const"].n == 16 * NCONSTD
                    po_box = [0]

                    def prep_out():
                        pi = po_box[0]
                        if pi >= NPREP:
                            return
                        mat, blki = PREP_CHUNKS[pi]
                        dst = {"g": gwbT, "u": uwbT, "d": dwbT}[mat]
                        w_ = II if mat in ("g", "u") else D
                        sm["s_ps"].wait(sy, ("sub", pi))
                        sm[_po_sem(pi)].inc(
                            sy.dma_start(out=dst[blki * 128:(blki + 1) * 128, :],
                                         in_=pstg[pi % 2][:, 0:w_]),
                            key=("po", pi), q=16)
                        po_box[0] += 1

                    for t, (u0, St, wm) in enumerate(TILES):
                        for ec in range(EC):
                            ge = t * EC + ec
                            if ge >= 2:
                                pt, pe = divmod(ge - 2, EC)
                                sm["s_r"].wait(sy, ("r", pt, pe))
                            sm[f"s_uch{ec % 2}"].inc(
                                sy.dma_start(
                                    out=u_ch[ec % 2][:, 0:St],
                                    in_=u_in[ec * 128:(ec + 1) * 128, u0:u0 + St]),
                                key=("uch", t, ec), q=16)
                            for _ in range(3):
                                prep_out()
                    while po_box[0] < NPREP:
                        prep_out()
                    for nmm in ("s_pog0", "s_pog1", "s_pou0", "s_pou1",
                                "s_pod0", "s_pod1"):
                        sm[nmm].mark("fin")

                @blk.gpsimd
                def _(gp):
                    pi_box = [0]

                    def prep_one():
                        pi = pi_box[0]
                        if pi >= NPREP:
                            return
                        mat, blki = PREP_CHUNKS[pi]
                        src = {"g": gw, "u": uwq, "d": dwq}[mat]
                        w_ = II if mat in ("g", "u") else D
                        if pi >= 2:
                            sm[_po_sem(pi - 2)].wait(gp, ("po", pi - 2))
                        sm[f"s_pi{pi % 2}"].inc(
                            gp.dma_start(out=pstg[pi % 2][:, 0:w_],
                                         in_=src[blki * 128:(blki + 1) * 128, :]),
                            key=("pi", pi), q=16)
                        pi_box[0] += 1

                    for t, (u0, St, wm) in enumerate(TILES):
                        if t > 0:
                            sm["s_prj"].wait(gp, ("last", t - 1, EC - 1))
                        sm["s_ubf"].inc(
                            gp.dma_start(
                                out=u_bf3[:, :, 0:St],
                                in_=u_in[:, u0:u0 + St]
                                .rearrange("(k p) s -> p k s", p=128)),
                            key=("ubf", t), q=16)
                        for ec in range(EC):
                            ge = t * EC + ec
                            if ge >= 2:
                                pt, pe = divmod(ge - 2, EC)
                                sm["s_prj"].wait(gp, ("last", pt, pe))
                            sm[f"s_ws{ec % 2}"].inc(
                                gp.dma_start(
                                    out=wstat[ec % 2][:],
                                    in_=wmix[ec * 128:(ec + 1) * 128, :]),
                                key=("ws", t, ec), q=16)
                            for _ in range(3):
                                prep_one()
                    while pi_box[0] < NPREP:
                        prep_one()

                @blk.tensor
                def _(te):
                    sm["s_const"].wait(te, "consts")
                    sm["s_vinit"].wait(te, "vinit")
                    bankP = {0: None, 1: None, 2: None}

                    def emit_stats(te_, t_, ec_):
                        if ec_ == 0 and t_ > 0:
                            sm["s_rv"].wait(te_, ("rv", t_ - 1))
                        sm["s_r"].wait(te_, ("r", t_, ec_))
                        sm["s_sq"].wait(te_, ("sq", t_, ec_))
                        te_.matmul(psSm[:], onesc_t[:], r_t3[:, ec_, :],
                                   start=(ec_ == 0), stop=(ec_ == EC - 1))
                        sm["s_st"].inc(
                            te_.matmul(psSq[:], onesc_t[:], spA[ec_ % 2][:, 0:512],
                                       start=(ec_ == 0), stop=(ec_ == EC - 1)),
                            key=("st", t_, ec_))

                    for t, (u0, St, wm) in enumerate(TILES):
                        rngs = _rngs(wm)
                        sm["s_ubf"].wait(te, ("ubf", t))
                        for ec in range(EC):
                            if ec >= 2:
                                emit_stats(te, t, ec - 2)
                            sm[f"s_ws{ec % 2}"].wait(te, ("ws", t, ec))
                            for ri, (off, W_) in enumerate(rngs):
                                for j in range(3):
                                    if bankP[j] is not None:
                                        s, k2 = bankP[j]
                                        sm[s].wait(te, k2)
                                    mm = None
                                    for dc in range(EC):
                                        mm = te.matmul(
                                            psP[j][:, 0:W_],
                                            wstat4[ec % 2][:, j, dc, :],
                                            u_bf3[:, dc, off:off + W_],
                                            start=(dc == 0), stop=(dc == EC - 1))
                                    sm["s_prj"].inc(mm, key=("prj", t, ec, ri, j))
                                    last_ri = ri == len(rngs) - 1
                                    bankP[j] = {
                                        0: ("s_sp", ("sp", t, ec, ri)),
                                        1: ("s_w", ("w", t, ec, ri)),
                                        2: (("s_cE", ("cE", t, ec)) if last_ri
                                            else ("s_prj", ("prj", t, ec, ri, 2))),
                                    }[j]
                            sm["s_prj"].mark(("last", t, ec))
                        emit_stats(te, t, EC - 2)
                        emit_stats(te, t, EC - 1)
                        sm["s_rv"].wait(te, ("rv", t))
                        sm["s_rs"].wait(te, ("rs", t))
                        if t > 0:
                            sm["s_x1"].wait(te, ("x1", t - 1, EC - 1))
                        te.matmul(psX[:], onesr_t[:], mrow[:], start=True, stop=True)
                        sm["s_bc"].inc(
                            te.matmul(psIV[:], onesr_t[:], vrow[:],
                                      start=True, stop=True),
                            key=("bc", t))

                @blk.scalar
                def _(se):
                    ss = sm["s_sself"]

                    def stag(instr):
                        ss.inc(instr)
                        return ss.n

                    def swait(v):
                        se.wait_ge(ss.h, v)

                    def stag_noinstr():
                        ss.n += 1
                        return ss.n

                    sm["s_const"].wait(se, "consts")
                    v = stag(se.activation(out=spl_t[:], in_=araw_t[:], func=AF.Abs))
                    swait(v)
                    v = stag(se.activation(out=A_t[:], in_=spl_t[:], func=AF.Exp,
                                           scale=-1.0))
                    swait(v)
                    v = stag(se.activation(out=spl_t[:], in_=A_t[:], func=AF.Ln,
                                           bias=1.0))
                    swait(v)
                    sm["s_sinit"].inc(
                        se.activation(out=A_t[:], in_=araw_t[:], func=AF.Relu),
                        key="sinit")

                    def sp_blk(t, ec):
                        rngs = _rngs(TILES[t][2])
                        ge = t * EC + ec
                        for ri, (off, W_) in enumerate(rngs):
                            sm["s_prj"].wait(se, ("prj", t, ec, ri, 0))
                            if ge >= 2:
                                pt, pe = divmod(ge - 2, EC)
                                sm["s_st"].wait(se, ("st", pt, pe))
                            bcol = bdt_t[:, ec:ec + 1]
                            v = stag(se.activation(
                                out=spA[ec % 2][:, off:off + W_],
                                in_=psP[0][:, 0:W_], func=AF.Exp, bias=bcol))
                            swait(v)
                            sm["s_sp"].inc(
                                se.activation(out=spA[ec % 2][:, off:off + W_],
                                              in_=spA[ec % 2][:, off:off + W_],
                                              func=AF.Ln, bias=1.0),
                                key=("sp", t, ec, ri))

                    def ce_blk(t, ec):
                        rngs = _rngs(TILES[t][2])
                        ge = t * EC + ec
                        sm["s_prj"].wait(se, ("prj", t, ec, len(rngs) - 1, 2))
                        if ge >= 2:
                            pt, pe = divmod(ge - 2, EC)
                            sm["s_r"].wait(se, ("r", pt, pe))
                        sm["s_cE"].inc(
                            se.activation(out=c_b[ec % 2][:],
                                          in_=psP[2][:, 0:512],
                                          func=AF.Identity,
                                          bias=bc_t[:, ec:ec + 1]),
                            key=("cE", t, ec))

                    for t, (u0, St, wm) in enumerate(TILES):
                        rngs = _rngs(wm)
                        for ec in range(EC):
                            ge = t * EC + ec
                            if ec == 0:
                                sp_blk(t, 0)
                            ce_blk(t, ec)
                            if ec + 1 < EC:
                                sp_blk(t, ec + 1)
                            sm["s_vinit"].wait(se, "vinit")
                            sm["s_w"].wait(se, ("w", t, ec, len(rngs) - 1))
                            for n in range(N):
                                gn = ge * N + n
                                if gn >= 4:
                                    pt2, rest = divmod(gn - 4, EC * N)
                                    pe2, pn2 = divmod(rest, N)
                                    sm["s_scan"].wait(se, ("scan", pt2, pe2, pn2))
                                sm["s_ab"].inc(
                                    se.activation(
                                        out=ab_b[n % 4][:, 0:St],
                                        in_=dt_b[ec % 2][:, 0:St], func=AF.Exp,
                                        scale=A_t[:, ec * N + n:ec * N + n + 1]),
                                    key=("ab", t, ec, n))
                            sm["s_r"].wait(se, ("r", t, ec))
                            sm["s_sq"].inc(
                                se.activation(out=spA[ec % 2][:, 0:512],
                                              in_=r_t3[:, ec, :], func=AF.Square),
                                key=("sq", t, ec))
                            tm = t * 512
                            if stage == 1:
                                sm[f"s_dbg{ec % 2}"].inc(
                                    se.dma_start(
                                        out=out[ec * 128:(ec + 1) * 128,
                                                tm:tm + 512],
                                        in_=r_t3[:, ec, :]),
                                    key=("dbg", t, ec), q=16)
                        sm["s_rv"].wait(se, ("rv", t))
                        v = stag(se.activation(out=vrow[:], in_=vrow[:], func=AF.Ln,
                                               bias=1e-5))
                        swait(v)
                        sm["s_rs"].inc(
                            se.activation(out=vrow[:], in_=vrow[:], func=AF.Exp,
                                          scale=-0.5),
                            key=("rs", t))
                        if stage >= 2:
                            tm = t * 512
                            for ec in range(EC):
                                ge = t * EC + ec
                                sm["s_x1"].wait(se, ("x1", t, ec))
                                if ge >= 2:
                                    pt, pe = divmod(ge - 2, EC)
                                    sm[f"s_x1w{(ge - 2) % 4}"].wait(se,
                                                                   ("x1w", pt, pe))
                                cast_i = se.activation(out=x1c_b[ec % 2][:],
                                                       in_=x1_b[ec % 2][:],
                                                       func=AF.Copy)
                                sm["s_x1b"].inc(cast_i, key=("x1b", t, ec))
                                se.wait_ge(sm["s_x1b"].h, sm["s_x1b"].n)
                                sm[f"s_x1w{ge % 4}"].inc(
                                    se.dma_start(
                                        out=x1f[ec * 128:(ec + 1) * 128,
                                                tm:tm + 512],
                                        in_=x1_b[ec % 2][:]), q=16)
                                sm[f"s_x1w{ge % 4}"].inc(
                                    se.dma_start(
                                        out=x1bd[ec * 128:(ec + 1) * 128,
                                                 tm:tm + 512],
                                        in_=x1c_b[ec % 2][:]),
                                    key=("x1w", t, ec), q=16)
                                if stage == 2:
                                    sm[f"s_dbg{ec % 2}"].inc(
                                        se.dma_start(
                                            out=out[ec * 128:(ec + 1) * 128,
                                                    tm:tm + 512],
                                            in_=x1_b[ec % 2][:]),
                                        key=("dbg", t, ec), q=16)
                        if stage < 3:
                            sm["s_dbg0"].mark(("dbgt", t))
                            sm["s_dbg1"].mark(("dbgt", t))
                    if stage >= 3:
                        for sl in range(4):
                            sm[f"s_x1w{sl}"].mark("fin")
                    if stage < 3:
                        sm["s_dbg0"].wait(se, ("dbgt", NT - 1))
                        sm["s_dbg1"].wait(se, ("dbgt", NT - 1))

                @blk.vector
                def _(ve):
                    vs = sm["s_vself"]

                    def vtag(instr):
                        vs.inc(instr)
                        return vs.n

                    def vwait(v):
                        ve.wait_ge(vs.h, v)

                    sm["s_const"].wait(ve, "consts")
                    sm["s_sinit"].wait(ve, "sinit")
                    v = vtag(ve.tensor_add(A_t[:], A_t[:], spl_t[:]))
                    vwait(v)
                    vtag(ve.tensor_scalar_mul(A_t[:], A_t[:], -1.0))
                    vtag(ve.tensor_mul(cbb_t[:], cbb_t[:], cb2_t[:]))
                    vtag(ve.tensor_scalar_add(dp1_t[:], dp1_t[:], 1.0))
                    vtag(ve.memset(onesc_t[:], 1.0))
                    vwait(vs.n)
                    sm["s_vinit"].inc(ve.memset(onesr_t[:], 1.0), key="vinit")
                    sm["s_vinit"].wait(ve, "vinit")
                    vsub_box = [0]

                    def prep_sub():
                        pi = vsub_box[0]
                        if pi >= NPREP:
                            return
                        mat, _b = PREP_CHUNKS[pi]
                        w_ = II if mat in ("g", "u") else D
                        sm[f"s_pi{pi % 2}"].wait(ve, ("pi", pi))
                        sm["s_ps"].inc(
                            ve.tensor_scalar(out=pstg[pi % 2][:, 0:w_],
                                             in0=pstg[pi % 2][:, 0:w_],
                                             scalar1=-128.0, scalar2=None,
                                             op0=ALU.add),
                            key=("sub", pi))
                        vsub_box[0] += 1
                    for t, (u0, St, wm) in enumerate(TILES):
                        rngs = _rngs(wm)
                        if t > 0:
                            vwait(vs.n)
                            if stage < 3:
                                sm["s_dbg0"].wait(ve, ("dbgt", t - 1))
                                sm["s_dbg1"].wait(ve, ("dbgt", t - 1))
                        for ec in range(EC):
                            ge = t * EC + ec
                            for _ in range(3):
                                prep_sub()
                            for ri, (off, W_) in enumerate(rngs):
                                sm["s_sp"].wait(ve, ("sp", t, ec, ri))
                                if ri == 0 and ge >= 2:
                                    pt, pe = divmod(ge - 2, EC)
                                    sm["s_ab"].wait(ve, ("ab", pt, pe, N - 1))
                                v = vtag(ve.tensor_scalar(
                                    out=dt_b[ec % 2][:, off:off + W_],
                                    in0=spA[ec % 2][:, off:off + W_],
                                    scalar1=eps4_t[:, 1:2], scalar2=None,
                                    op0=ALU.add))
                                sm["s_prj"].wait(ve, ("prj", t, ec, ri, 1))
                                vwait(v)
                                v = vtag(ve.scalar_tensor_tensor(
                                    out=w_b[ec % 2][:, off:off + W_],
                                    in0=psP[1][:, 0:W_], scalar=bb_t[:, ec:ec + 1],
                                    in1=dt_b[ec % 2][:, off:off + W_],
                                    op0=ALU.add, op1=ALU.mult))
                                sm[f"s_uch{ec % 2}"].wait(ve, ("uch", t, ec))
                                vwait(v)
                                sm["s_w"].inc(
                                    ve.tensor_mul(w_b[ec % 2][:, off:off + W_],
                                                  w_b[ec % 2][:, off:off + W_],
                                                  u_ch[ec % 2][:, off:off + W_]),
                                    key=("w", t, ec, ri))
                            M0 = wm
                            last = None
                            for n in range(N):
                                sm["s_ab"].wait(ve, ("ab", t, ec, n))
                                init = (0.0 if t == 0
                                        else carry[:, ec * N + n:ec * N + n + 1])
                                sci = ve.tensor_tensor_scan(
                                    out=xp_b[n % 2][:, 0:St],
                                    data0=ab_b[n % 4][:, 0:St],
                                    data1=w_b[ec % 2][:, 0:St],
                                    initial=init, op0=ALU.mult, op1=ALU.add)
                                sm["s_scan"].inc(sci, key=("scan", t, ec, n))
                                ve.wait_ge(sm["s_scan"].h, sm["s_scan"].n)
                                if n > 0:
                                    vwait(vs.n)
                                vtag(ve.tensor_copy(
                                    carry[:, ec * N + n:ec * N + n + 1],
                                    xp_b[n % 2][:, St - 1:St]))
                                if n == 0:
                                    last = ve.tensor_scalar(
                                        out=ya_b[ec % 2][:],
                                        in0=xp_b[0][:, M0:M0 + 512],
                                        scalar1=cbb_t[:, ec * N:ec * N + 1],
                                        scalar2=None, op0=ALU.mult)
                                else:
                                    last = ve.scalar_tensor_tensor(
                                        out=ya_b[ec % 2][:],
                                        in0=xp_b[n % 2][:, M0:M0 + 512],
                                        scalar=cbb_t[:, ec * N + n:ec * N + n + 1],
                                        in1=ya_b[ec % 2][:], op0=ALU.mult,
                                        op1=ALU.add)
                                vtag(last)

                            sm["s_cE"].wait(ve, ("cE", t, ec))
                            vwait(vs.n)
                            v = vtag(ve.tensor_mul(ya_b[ec % 2][:], ya_b[ec % 2][:],
                                                   c_b[ec % 2][:]))
                            if t > 0:
                                sm["s_st"].wait(ve, ("st", t - 1, ec))
                                sm["s_sq"].wait(ve, ("sq", t - 1, ec))
                            vwait(v)
                            sm["s_r"].inc(
                                ve.scalar_tensor_tensor(
                                    out=r_t3[:, ec, :],
                                    in0=u_ch[ec % 2][:, M0:M0 + 512],
                                    scalar=dp1_t[:, ec:ec + 1],
                                    in1=ya_b[ec % 2][:], op0=ALU.mult, op1=ALU.add),
                                key=("r", t, ec))
                        sm["s_st"].wait(ve, ("st", t, EC - 1))
                        v = vtag(ve.tensor_scalar(out=mrow[:], in0=psSm[:],
                                                  scalar1=1.0 / D,
                                                  scalar2=None, op0=ALU.mult))
                        vwait(v)
                        v = vtag(ve.tensor_mul(vrow[:], mrow[:], mrow[:]))
                        vwait(v)
                        sm["s_rv"].inc(
                            ve.scalar_tensor_tensor(out=vrow[:], in0=psSq[:],
                                                    scalar=1.0 / D, in1=vrow[:],
                                                    op0=ALU.mult, op1=ALU.subtract),
                            key=("rv", t))
                        if stage >= 2:
                            sm["s_bc"].wait(ve, ("bc", t))
                            for ec in range(EC):
                                ge = t * EC + ec
                                if ge >= 2:
                                    pt, pe = divmod(ge - 2, EC)
                                    sm["s_x1b"].wait(ve, ("x1b", pt, pe))
                                    sm[f"s_x1w{(ge - 2) % 4}"].wait(ve, ("x1w", pt, pe))
                                    if stage == 2:
                                        sm[f"s_dbg{pe % 2}"].wait(ve, ("dbg", pt, pe))
                                v = vtag(ve.tensor_sub(x1_b[ec % 2][:],
                                                       r_t3[:, ec, :], psX[:]))
                                vwait(v)
                                sm["s_x1"].inc(
                                    ve.tensor_mul(x1_b[ec % 2][:], x1_b[ec % 2][:],
                                                  psIV[:]),
                                    key=("x1", t, ec))
                        else:
                            for ec in range(EC):
                                sm["s_x1"].inc(ve.nop(), key=("x1", t, ec))
                    while vsub_box[0] < NPREP:
                        prep_sub()

        if stage >= 3:
            with ExitStack() as cf:
                fsb = lambda nm, sh, dt: cf.enter_context(nc.sbuf_tensor(nm, sh, dt))
                fps = lambda nm, sh: cf.enter_context(nc.psum_tensor(nm, sh, F32))
                x1n = fsb("x1n", [128, EC * 512], BF16)
                x1n3 = x1n[:].rearrange("p (k t) -> p k t", k=EC)
                gstat = [fsb(f"gstat{i}", [128, EC * 2 * 128], BF16) for i in range(2)]
                gstat4 = [g[:].rearrange("p (k i e) -> p k i e", k=EC, i=2)
                          for g in gstat]
                ustat = [fsb(f"ustat{i}", [128, EC * 2 * 128], BF16) for i in range(2)]
                ustat4 = [u_[:].rearrange("p (k i e) -> p k i e", k=EC, i=2)
                          for u_ in ustat]
                h_t = fsb("h_t", [128, IC * 512], BF16)
                h_t3 = h_t[:].rearrange("p (k t) -> p k t", k=IC)
                dstat = [fsb(f"dstat{i}", [128, IC * 128], BF16) for i in range(2)]
                dstat3 = [d[:].rearrange("p (k e) -> p k e", k=IC) for d in dstat]
                sg_b = [fsb(f"sg_b{i}", [128, 512], F32) for i in range(2)]
                x1f_ch = [fsb(f"x1f_ch{i}", [128, 512], F32) for i in range(2)]
                r2_b = [fsb(f"r2_b{i}", [128, 512], F32) for i in range(2)]
                sq2_b = [fsb(f"sq2_b{i}", [128, 512], F32) for i in range(2)]
                r2r_b = [fsb(f"r2r_b{i}", [128, 512], F32) for i in range(2)]
                x2_b = [fsb(f"x2_b{i}", [128, 512], F32) for i in range(2)]
                psG = [fps(f"psG{j}", [128, 512]) for j in range(2)]
                psU = fps("psU", [128, 512])
                psD = fps("psD", [128, 512])
                psS2m = fps("psS2m", [1, 512])
                psS2q = fps("psS2q", [1, 512])
                psM2 = fps("psM2", [128, 512])
                psV2 = fps("psV2", [128, 512])

                with nc.Block() as fbk:
                    @fbk.gpsimd
                    def _(gp):
                        for ts in range(4):
                            tm = ts * 512
                            if ts == 0:
                                for sl in range(4):
                                    sm[f"s_x1w{sl}"].wait(gp, "fin")
                                for nmm in ("s_pog0", "s_pog1", "s_pou0",
                                            "s_pou1", "s_pod0", "s_pod1"):
                                    sm[nmm].wait(gp, "fin")
                            else:
                                sm["s_fu"].wait(gp, ("fu", ts - 1, IC - 1))
                            sm["s_fx"].inc(
                                gp.dma_start(
                                    out=x1n3[:, :, :],
                                    in_=x1bd[:, tm:tm + 512]
                                    .rearrange("(k p) t -> p k t", p=128)),
                                key=("fx", ts), q=16)
                            for icg in range(IC // 2):
                                rot = icg % 2
                                gi = ts * (IC // 2) + icg
                                if gi >= 2:
                                    pts, pik = divmod(gi - 2, IC // 2)
                                    sm["s_fu"].wait(gp, ("fu", pts, pik * 2 + 1))
                                i0 = icg * 256
                                sm[f"s_fgs{rot}"].inc(
                                    gp.dma_start(
                                        out=gstat4[rot][:, :, :, :],
                                        in_=gwbT[:, i0:i0 + 256].rearrange(
                                            "(k p) (i e) -> p k i e", p=128, i=2)),
                                    key=("fgsg", ts, icg), q=16)
                                sm[f"s_fus{rot}"].inc(
                                    gp.dma_start(
                                        out=ustat4[rot][:, :, :, :],
                                        in_=uwbT[:, i0:i0 + 256].rearrange(
                                            "(k p) (i e) -> p k i e", p=128, i=2)),
                                    key=("fgs", ts, icg), q=16)

                    @fbk.sync
                    def _(sy):
                        for sl in range(4):
                            sm[f"s_x1w{sl}"].wait(sy, "fin")
                        sm["s_pod0"].wait(sy, "fin")
                        sm["s_pod1"].wait(sy, "fin")
                        for ts in range(4):
                            tm = ts * 512
                            for hc in range(EC):
                                gh = ts * EC + hc
                                if gh >= 2:
                                    pts, phc = divmod(gh - 2, EC)
                                    sm["s_fdm"].wait(sy, ("fdm", pts, phc))
                                sm[f"s_fds{hc % 2}"].inc(
                                    sy.dma_start(
                                        out=dstat3[hc % 2][:, :, :],
                                        in_=dwbT[:, hc * 128:(hc + 1) * 128]
                                        .rearrange("(k p) e -> p k e", p=128)),
                                    key=("fds", ts, hc), q=16)
                                if gh >= 2:
                                    pts, phc = divmod(gh - 2, EC)
                                    sm["s_fr2"].wait(sy, ("fr2", pts, phc))
                                sm[f"s_fx1r{hc % 2}"].inc(
                                    sy.dma_start(
                                        out=x1f_ch[hc % 2][:],
                                        in_=x1f[hc * 128:(hc + 1) * 128,
                                                tm:tm + 512]),
                                    key=("fx1r", ts, hc), q=16)
                                if hc >= 1:
                                    sm["s_fr2"].wait(sy, ("fr2", ts, hc - 1))
                                    sm[f"s_fr2w{(hc - 1) % 2}"].inc(
                                        sy.dma_start(
                                            out=r2f[(hc - 1) * 128:hc * 128,
                                                    tm:tm + 512],
                                            in_=r2_b[(hc - 1) % 2][:]),
                                        key=("fr2w", ts, hc - 1), q=16)
                            sm["s_fr2"].wait(sy, ("fr2", ts, EC - 1))
                            sm[f"s_fr2w{(EC - 1) % 2}"].inc(
                                sy.dma_start(
                                    out=r2f[(EC - 1) * 128:EC * 128, tm:tm + 512],
                                    in_=r2_b[(EC - 1) % 2][:]),
                                key=("fr2w", ts, EC - 1), q=16)
                            sm["s_fr2w0"].mark(("r2wt", ts))
                            sm["s_fr2w1"].mark(("r2wt", ts))
                            for hc in range(EC):
                                gh = ts * EC + hc
                                if gh >= 2:
                                    pts, phc = divmod(gh - 2, EC)
                                    sm["s_fx2"].wait(sy, ("fx2", pts, phc))
                                sm["s_fr2w0"].wait(sy, ("r2wt", ts))
                                sm["s_fr2w1"].wait(sy, ("r2wt", ts))
                                sm[f"s_fr2r{hc % 2}"].inc(
                                    sy.dma_start(
                                        out=r2r_b[hc % 2][:],
                                        in_=r2f[hc * 128:(hc + 1) * 128,
                                                tm:tm + 512]),
                                    key=("fr2r", ts, hc), q=16)
                                sm["s_fx2"].wait(sy, ("fx2", ts, hc))
                                sm[f"s_fo{hc % 2}"].inc(
                                    sy.dma_start(
                                        out=out[hc * 128:(hc + 1) * 128,
                                                tm:tm + 512],
                                        in_=x2_b[hc % 2][:]),
                                    key=("fo", ts, hc), q=16)
                        sy.wait_ge(sm["s_fo0"].h, sm["s_fo0"].n)
                        sy.wait_ge(sm["s_fo1"].h, sm["s_fo1"].n)

                    @fbk.tensor
                    def _(te):
                        def stats2(ts_, hc_):
                            if hc_ == 0 and ts_ > 0:
                                sm["s_frv"].wait(te, ("frv", ts_ - 1))
                            sm["s_fr2"].wait(te, ("fr2", ts_, hc_))
                            sm["s_fsq"].wait(te, ("fsq", ts_, hc_))
                            te.matmul(psS2m[:], onesc_t[:], r2_b[hc_ % 2][:],
                                      start=(hc_ == 0), stop=(hc_ == EC - 1))
                            sm["s_fst"].inc(
                                te.matmul(psS2q[:], onesc_t[:], sq2_b[hc_ % 2][:],
                                          start=(hc_ == 0), stop=(hc_ == EC - 1)),
                                key=("fst", ts_, hc_))

                        for ts in range(4):
                            sm["s_fx"].wait(te, ("fx", ts))
                            for icg in range(IC // 2):
                                rot = icg % 2
                                sm[f"s_fgs{rot}"].wait(te, ("fgsg", ts, icg))
                                sm[f"s_fus{rot}"].wait(te, ("fgs", ts, icg))
                                for ii in range(2):
                                    ic = icg * 2 + ii
                                    gi2 = ts * IC + ic
                                    if gi2 >= 2:
                                        pts, pic = divmod(gi2 - 2, IC)
                                        sm["s_fsg"].wait(te, ("fsg", pts, pic))
                                    mm = None
                                    for k in range(EC):
                                        mm = te.matmul(
                                            psG[ic % 2][:],
                                            gstat4[rot][:, k, ii, :],
                                            x1n3[:, k, :],
                                            start=(k == 0), stop=(k == EC - 1))
                                    sm["s_fg"].inc(mm, key=("fg", ts, ic))
                                    if gi2 >= 1:
                                        pts, pic = divmod(gi2 - 1, IC)
                                        sm["s_fh"].wait(te, ("fh", pts, pic))
                                    mm = None
                                    for k in range(EC):
                                        mm = te.matmul(
                                            psU[:],
                                            ustat4[rot][:, k, ii, :],
                                            x1n3[:, k, :],
                                            start=(k == 0), stop=(k == EC - 1))
                                    sm["s_fu"].inc(mm, key=("fu", ts, ic))
                            for hc in range(EC):
                                if hc >= 2:
                                    stats2(ts, hc - 2)
                                sm[f"s_fds{hc % 2}"].wait(te, ("fds", ts, hc))
                                sm["s_fh"].wait(te, ("fh", ts, IC - 1))
                                if hc >= 1:
                                    sm["s_fr2"].wait(te, ("fr2", ts, hc - 1))
                                elif ts > 0:
                                    sm["s_fr2"].wait(te, ("fr2", ts - 1, EC - 1))
                                mm = None
                                for k in range(IC):
                                    mm = te.matmul(
                                        psD[:], dstat3[hc % 2][:, k, :],
                                        h_t3[:, k, :],
                                        start=(k == 0), stop=(k == IC - 1))
                                sm["s_fdm"].inc(mm, key=("fdm", ts, hc))
                            stats2(ts, EC - 2)
                            stats2(ts, EC - 1)
                            sm["s_frv"].wait(te, ("frv", ts))
                            sm["s_frs"].wait(te, ("frs", ts))
                            if ts > 0:
                                sm["s_fx2"].wait(te, ("fx2", ts - 1, EC - 1))
                            te.matmul(psM2[:], onesr_t[:], mrow[:],
                                      start=True, stop=True)
                            sm["s_fbc"].inc(
                                te.matmul(psV2[:], onesr_t[:], vrow[:],
                                          start=True, stop=True),
                                key=("fbc", ts))

                    @fbk.scalar
                    def _(se):
                        ss = sm["s_sself"]
                        for nmm in ("s_pog0", "s_pog1", "s_pou0", "s_pou1",
                                    "s_pod0", "s_pod1"):
                            sm[nmm].wait(se, "fin")
                        for ts in range(4):
                            for ic in range(IC):
                                gi2 = ts * IC + ic
                                sm["s_fg"].wait(se, ("fg", ts, ic))
                                if gi2 >= 2:
                                    pts, pic = divmod(gi2 - 2, IC)
                                    sm["s_fh"].wait(se, ("fh", pts, pic))
                                sm["s_fsg"].inc(
                                    se.activation(out=sg_b[ic % 2][:],
                                                  in_=psG[ic % 2][:],
                                                  func=AF.Silu,
                                                  scale=gs_t[:, ic:ic + 1]),
                                    key=("fsg", ts, ic))
                            for hc in range(EC):
                                gh = ts * EC + hc
                                sm["s_fr2"].wait(se, ("fr2", ts, hc))
                                if gh >= 2:
                                    pts, phc = divmod(gh - 2, EC)
                                    sm["s_fst"].wait(se, ("fst", pts, phc))
                                sm["s_fsq"].inc(
                                    se.activation(out=sq2_b[hc % 2][:],
                                                  in_=r2_b[hc % 2][:],
                                                  func=AF.Square),
                                    key=("fsq", ts, hc))
                            sm["s_frv"].wait(se, ("frv", ts))
                            ss.inc(se.activation(out=vrow[:], in_=vrow[:],
                                                 func=AF.Ln, bias=1e-5))
                            se.wait_ge(ss.h, ss.n)
                            sm["s_frs"].inc(
                                se.activation(out=vrow[:], in_=vrow[:],
                                              func=AF.Exp, scale=-0.5),
                                key=("frs", ts))

                    @fbk.vector
                    def _(ve):
                        vs = sm["s_vself"]
                        for nmm in ("s_pog0", "s_pog1", "s_pou0", "s_pou1",
                                    "s_pod0", "s_pod1"):
                            sm[nmm].wait(ve, "fin")
                        for ts in range(4):
                            for ic in range(IC):
                                gi2 = ts * IC + ic
                                sm["s_fu"].wait(ve, ("fu", ts, ic))
                                sm["s_fsg"].wait(ve, ("fsg", ts, ic))
                                if ts > 0 and ic == 0:
                                    sm["s_fdm"].wait(ve, ("fdm", ts - 1, EC - 1))
                                sm["s_fh"].inc(
                                    ve.scalar_tensor_tensor(
                                        out=h_t3[:, ic, :], in0=psU[:],
                                        scalar=us_t[:, ic:ic + 1],
                                        in1=sg_b[ic % 2][:],
                                        op0=ALU.mult, op1=ALU.mult),
                                    key=("fh", ts, ic))
                            for hc in range(EC):
                                gh = ts * EC + hc
                                sm["s_fdm"].wait(ve, ("fdm", ts, hc))
                                sm[f"s_fx1r{hc % 2}"].wait(ve, ("fx1r", ts, hc))
                                if gh >= 2:
                                    pts, phc = divmod(gh - 2, EC)
                                    sm[f"s_fr2w{phc % 2}"].wait(ve,
                                                                ("fr2w", pts, phc))
                                    sm["s_fst"].wait(ve, ("fst", pts, phc))
                                sm["s_fr2"].inc(
                                    ve.scalar_tensor_tensor(
                                        out=r2_b[hc % 2][:], in0=psD[:],
                                        scalar=ds_t[:, hc:hc + 1],
                                        in1=x1f_ch[hc % 2][:],
                                        op0=ALU.mult, op1=ALU.add),
                                    key=("fr2", ts, hc))
                            sm["s_fst"].wait(ve, ("fst", ts, EC - 1))
                            v = vs.n + 1
                            vs.inc(ve.tensor_scalar(out=mrow[:], in0=psS2m[:],
                                                    scalar1=1.0 / D, scalar2=None,
                                                    op0=ALU.mult))
                            ve.wait_ge(vs.h, v)
                            v = vs.n + 1
                            vs.inc(ve.tensor_mul(vrow[:], mrow[:], mrow[:]))
                            ve.wait_ge(vs.h, v)
                            sm["s_frv"].inc(
                                ve.scalar_tensor_tensor(out=vrow[:], in0=psS2q[:],
                                                        scalar=1.0 / D, in1=vrow[:],
                                                        op0=ALU.mult,
                                                        op1=ALU.subtract),
                                key=("frv", ts))
                            sm["s_fbc"].wait(ve, ("fbc", ts))
                            for hc in range(EC):
                                gh = ts * EC + hc
                                sm[f"s_fr2r{hc % 2}"].wait(ve, ("fr2r", ts, hc))
                                if gh >= 2:
                                    pts, phc = divmod(gh - 2, EC)
                                    sm[f"s_fo{phc % 2}"].wait(ve, ("fo", pts, phc))
                                v = vs.n + 1
                                vs.inc(ve.tensor_sub(x2_b[hc % 2][:],
                                                     r2r_b[hc % 2][:], psM2[:]))
                                ve.wait_ge(vs.h, v)
                                sm["s_fx2"].inc(
                                    ve.tensor_mul(x2_b[hc % 2][:], x2_b[hc % 2][:],
                                                  psV2[:]),
                                    key=("fx2", ts, hc))

        tables = {name: dict(s.keys) for name, s in sm.items()}
    return nc, tables


def build(stage=3):
    _, tables = build_nc(stage=stage, table=None)
    nc, t2 = build_nc(stage=stage, table=tables)
    assert t2 == tables, "nondeterministic build"
    return nc, tables


def make_inmaps(inputs):
    u = np.asarray(inputs["L0_attn_u"], np.float32)

    def col(v):
        return np.ascontiguousarray(np.asarray(v, np.float32).reshape(EC, 128).T)

    def colN(v):
        v = np.asarray(v, np.float32).reshape(EC, 128, N)
        return np.ascontiguousarray(v.transpose(1, 0, 2).reshape(128, EC * N))

    wmix_arr = np.stack([np.asarray(inputs["L0_attn_Wdt"], np.float32),
                         np.asarray(inputs["L0_attn_Wb"], np.float32),
                         np.asarray(inputs["L0_attn_Wc"], np.float32)])
    wmix_arr = np.ascontiguousarray(
        wmix_arr.reshape(3, EC, 128, EC, 128).transpose(3, 2, 0, 1, 4)
        .reshape(D, 3 * EC * 128))
    com = {
        "wmix": wmix_arr,
        "bdt": col(inputs["L0_attn_bdt"]), "bb": col(inputs["L0_attn_bb"]),
        "bc": col(inputs["L0_attn_bc"]), "araw": colN(inputs["L0_attn_a_raw"]),
        "bbase": colN(inputs["L0_attn_B_base"]),
        "cbase": colN(inputs["L0_attn_C_base"]),
        "dvec": col(inputs["L0_attn_D"]),
        "cin": np.zeros((1, 8), np.float32),
        "gw": np.ascontiguousarray(np.asarray(inputs["L0_ffn_gate_w"], np.int32).T),
        "uw": np.ascontiguousarray(np.asarray(inputs["L0_ffn_up_w"], np.int32).T),
        "dw": np.ascontiguousarray(np.asarray(inputs["L0_ffn_down_w"], np.int32).T),
        "gs": np.ascontiguousarray(
            np.asarray(inputs["L0_ffn_gate_scale"], np.float32).reshape(IC, 128).T),
        "us": np.ascontiguousarray(
            np.asarray(inputs["L0_ffn_up_scale"], np.float32).reshape(IC, 128).T),
        "ds": np.ascontiguousarray(
            np.asarray(inputs["L0_ffn_down_scale"], np.float32).reshape(EC, 128).T),
        "ccol": np.ascontiguousarray(
            np.tile(np.array([[-128.0, 1e-4]], np.float32), (128, 1))),
    }
    maps = []
    for c in range(8):
        b, half = c // 2, c % 2
        useg = np.zeros((WARM + TOK, D), np.float32)
        start = half * TOK
        useg[WARM:] = u[b, start:start + TOK]
        if half == 1:
            useg[:WARM] = u[b, start - WARM:start]
        maps.append({"u": np.ascontiguousarray(useg.T), **com})
    return maps


def kernel(**inputs):
    from concourse.bass_utils import run_bass_kernel_spmd

    maps = make_inmaps(inputs)
    nc, _tables = build(stage=3)
    res = run_bass_kernel_spmd(nc, maps, core_ids=list(range(8)))
    first = np.asarray(res.results[0]["out"])
    out = np.empty((4, 4096, 2048), first.dtype)
    for c in range(8):
        b, half = c // 2, c % 2
        got = np.asarray(res.results[c]["out"])
        out[b, half * TOK:(half + 1) * TOK, :] = got.T
    return out
